# version 07
# baseline (speedup 1.0000x reference)
import numpy as np
import ml_dtypes
import concourse.bass as bass
import concourse.mybir as mybir
from concourse.bass_utils import run_bass_kernel_spmd

F32 = mybir.dt.float32
BF16 = mybir.dt.bfloat16
AF = mybir.ActivationFunctionType
ALU = mybir.AluOpType
AX = mybir.AxisListType
BF = ml_dtypes.bfloat16

NCORES = 8
E = 1024
NH = 16
HD = 64
MF = 256
EPS_DENOM = 1e-6 * 256.0
LN_EPS = 1e-5
TWO_PI = float(2.0 * np.pi)
CH = 512


def _hadamard64():
    h = np.array([[1.0]], dtype=np.float64)
    while h.shape[0] < 64:
        h = np.block([[h, h], [h, -h]])
    return h


PERM = np.concatenate([np.arange(0, 64, 2), np.arange(1, 64, 2)])
H64_NP = _hadamard64().astype(np.float32)
HP_NP = np.ascontiguousarray(H64_NP[PERM, :])

ENGINES = ("t", "v", "s", "g", "y")
_BODY = {"t": "tensor", "v": "vector", "s": "scalar", "g": "gpsimd", "y": "sync"}


class Builder:
    NDMA = 8

    def __init__(self, nc):
        self.nc = nc
        self.sems = {}
        self.counts = {}
        self.progs = {en: [] for en in ENGINES}
        self._sem_ctxs = []
        self._dma_rr = {en: 0 for en in ENGINES}
        for en in ENGINES:
            for name in [en] + [f"{en}D{k}" for k in range(self.NDMA)]:
                ctx = nc.semaphore(f"sem_{name}")
                self.sems[name] = ctx.__enter__()
                self._sem_ctxs.append(ctx)
                self.counts[name] = 0
        self.fenced = {en: {} for en in ENGINES}

    def init_consts(self, tile):
        self._ctile = tile
        self._cmap = {}
        self._cn = 0

    def const(self, value):
        if value in self._cmap:
            return self._cmap[value]
        ap = self._ctile[:, self._cn:self._cn + 1]
        self._cn += 1
        ev = self.op("v", "memset", ap, value)
        self.nc.const_aps.register(F32, value, ap)
        self._cmap[value] = ev
        return ev

    def _mkdeps(self, en, deps):
        flat = []
        for d in deps:
            if d is None:
                continue
            if isinstance(d, list):
                flat.extend(x for x in d if x is not None)
            else:
                flat.append(d)
        best = {}
        fen = self.fenced[en]
        for (n, c) in flat:
            if c <= 0:
                continue
            if fen.get(n, 0) >= c:
                continue
            if best.get(n, -1) < c:
                best[n] = c
        for n, c in best.items():
            fen[n] = c
        return tuple(best.items())

    def op(self, en, meth, *args, deps=(), inc=True, **kwargs):
        assert en in ENGINES and en != "y"
        if inc:
            self.counts[en] += 1
        my_inc = 1 if inc else 0
        sems = self.sems
        dd = self._mkdeps(en, deps)

        def emit(eng):
            for (dn, dc) in dd:
                eng.wait_ge(sems[dn], dc)
            inst = getattr(eng, meth)(*args, **kwargs)
            if my_inc:
                inst.then_inc(sems[en], 1)

        self.progs[en].append(emit)
        return (en, self.counts[en])

    def dma(self, en, out, in_, deps=(), **kwargs):
        k = self._dma_rr[en]
        self._dma_rr[en] = (k + 1) % self.NDMA
        dn_name = f"{en}D{k}"
        prev = (dn_name, self.counts[dn_name])
        self.counts[dn_name] += 16
        sems = self.sems
        dd = self._mkdeps(en, tuple(deps) + (prev,))

        def emit(eng):
            for (dn, dc) in dd:
                eng.wait_ge(sems[dn], dc)
            eng.dma_start(out=out, in_=in_, **kwargs).then_inc(sems[dn_name], 16)

        self.progs[en].append(emit)
        return (dn_name, self.counts[dn_name])

    def fence(self, en, deps):
        sems = self.sems
        dd = self._mkdeps(en, deps)
        if not dd:
            return

        def emit(eng):
            for (dn, dc) in dd:
                eng.wait_ge(sems[dn], dc)

        self.progs[en].append(emit)

    def now(self, en):
        return (en, self.counts[en])

    def allD(self, en):
        return [(f"{en}D{k}", self.counts[f"{en}D{k}"]) for k in range(self.NDMA)]

    def everything(self):
        evs = []
        for en in ENGINES:
            if en != "y":
                evs.append(self.now(en))
            evs.extend(self.allD(en))
        return evs

    def emit(self, block):
        for en in ENGINES:
            prog = self.progs[en]

            def runner(eng, prog=prog):
                for fn in prog:
                    fn(eng)

            getattr(block, _BODY[en])(runner)

    def close(self):
        for ctx in reversed(self._sem_ctxs):
            ctx.__exit__(None, None, None)


def dram_ap(t, offset, dims):
    return bass.AP(t, offset, [[s, c] for (s, c) in dims])


class Arena:
    def __init__(self, tensor, cols):
        self.t = tensor
        self.cap = cols
        self.off = 0
        self.marks = {}

    def alloc(self, cols):
        assert self.off + cols <= self.cap, f"arena overflow {self.off}+{cols}>{self.cap}"
        off = self.off
        self.off += cols
        return self.t[:, off:off + cols]

    def mark(self, name):
        self.marks[name] = self.off

    def reset(self, name):
        self.off = self.marks[name]


def build_program(L=8192, dbg=False):
    NCH = L // CH
    TB = min(1024, L)
    NBLK = L // TB
    NT128 = TB // 128

    nc = bass.Bass("TRN2", target_bir_lowering=False)

    x_ap = nc.dram_tensor("x", [L, E], F32, kind="ExternalInput").ap()
    wqkv_t = nc.dram_tensor("w_qkv", [3 * E, E], F32, kind="ExternalInput")
    wout_t = nc.dram_tensor("w_out", [E, E], F32, kind="ExternalInput")
    signs_t = nc.dram_tensor("sorf_signs", [NH, 4, 3, 64], F32, kind="ExternalInput")
    scales_t = nc.dram_tensor("sorf_scales", [NH, 4, 64], F32, kind="ExternalInput")
    rw_t = nc.dram_tensor("router_w", [8, E], F32, kind="ExternalInput")
    w1_t = nc.dram_tensor("w1", [8, 4096, E], F32, kind="ExternalInput")
    b1_t = nc.dram_tensor("b1", [8, 4096], F32, kind="ExternalInput")
    w2_t = nc.dram_tensor("w2", [8, E, 4096], F32, kind="ExternalInput")
    b2_t = nc.dram_tensor("b2", [8, E], F32, kind="ExternalInput")
    id64_ap = nc.dram_tensor("id64", [128, 128], F32, kind="ExternalInput").ap()
    h64_ap = nc.dram_tensor("h64c", [64, 64], F32, kind="ExternalInput").ap()
    hp_ap = nc.dram_tensor("hpc", [64, 64], F32, kind="ExternalInput").ap()
    cossin_ap = nc.dram_tensor("cossin", [64, L], BF16, kind="ExternalInput").ap()
    cossin_sw_ap = nc.dram_tensor("cossin_sw", [64, L], BF16, kind="ExternalInput").ap()

    out_ap = nc.dram_tensor("out", [L, E], F32, kind="ExternalOutput").ap()

    wqkv_bf = nc.dram_tensor("wqkv_bf", [3 * E, E], BF16)
    wout_bf = nc.dram_tensor("wout_bf", [E, E], BF16)
    b2_bf = nc.dram_tensor("b2_bf", [8, E], BF16)
    w1_bf = nc.dram_tensor("w1_bf", [8, 4096, E], BF16)
    w2_bf = nc.dram_tensor("w2_bf", [8, E, 4096], BF16)
    w1T_sc = nc.dram_tensor("w1T_sc", [8, 8, 4, 128, 1024], BF16)
    w2T_sc = nc.dram_tensor("w2T_sc", [8, 32, 128, 1024], BF16)
    xln_sc = nc.dram_tensor("xln_sc", [L, E], BF16)
    cat_sc = nc.dram_tensor("cat_sc", [L, E], BF16)
    out1_sc = nc.dram_tensor("out1_sc", [L, E], F32)
    x2ln_sc = nc.dram_tensor("x2ln_sc", [L, E], BF16)

    dbg_aps = {}
    if dbg == "light":
        dbg_aps["d_gatesA"] = nc.dram_tensor("d_gatesA", [NBLK, 128, 8 * NT128], F32,
                                             kind="ExternalOutput").ap()
        dbg_aps["d_x2A"] = nc.dram_tensor("d_x2A", [L, E], BF16,
                                          kind="ExternalOutput").ap()
        dbg_aps["d_oaccA"] = nc.dram_tensor("d_oaccA", [L, E], F32,
                                            kind="ExternalOutput").ap()
        dbg_aps["d_e0A"] = nc.dram_tensor("d_e0A", [1024, E], F32,
                                          kind="ExternalOutput").ap()
        dbg_aps["d_w1s"] = nc.dram_tensor("d_w1s", [128, 8192], BF16,
                                          kind="ExternalOutput").ap()
        dbg_aps["d_w2s"] = nc.dram_tensor("d_w2s", [128, 8192], BF16,
                                          kind="ExternalOutput").ap()
        dbg_aps["d_h1"] = nc.dram_tensor("d_h1", [128, 4096], BF16,
                                         kind="ExternalOutput").ap()
        dbg_aps["d_w1sc"] = nc.dram_tensor("d_w1sc", [128, 4096], BF16,
                                           kind="ExternalOutput").ap()
        dbg_aps["d_w1bf"] = nc.dram_tensor("d_w1bf", [1024, 128], BF16,
                                           kind="ExternalOutput").ap()
    if dbg is True:
        for name, shape, dt in [
            ("d_xln", [L, E], BF16), ("d_kq", [64, NH * CH], BF16),
            ("d_kphi", [128, NH * MF], BF16), ("d_S", [128, 32 * 65], F32),
            ("d_cat", [L, E], BF16), ("d_out1", [L, E], F32),
            ("d_gates", [128, 8 * NT128], F32), ("d_W", [64, NH * MF], BF16),
            ("d_x2ln", [L, E], BF16),
        ]:
            dbg_aps[name] = nc.dram_tensor(name, shape, dt, kind="ExternalOutput").ap()

    B = Builder(nc)
    sb_ctxs = []

    def sbuf(name, shape, dt):
        ctx = nc.sbuf_tensor(name, shape, dt)
        t = ctx.__enter__()
        sb_ctxs.append(ctx)
        return t

    cpool = sbuf("cpool", [128, 8], F32)
    id128 = sbuf("id128_sb", [128, 128], F32)
    id64 = id128[0:64, 0:64]
    h64 = sbuf("h64_sb", [64, 64], F32)
    hp = sbuf("hp_sb", [64, 64], F32)
    scN = sbuf("scN", [64, 192], F32)
    scP = sbuf("scP", [64, 192], F32)
    gcols = sbuf("gcols", [64, 64], F32)
    sctmp = sbuf("sctmp", [96, 64], F32)
    sperm = sbuf("sperm", [96, 64], F32)
    wtmp = sbuf("wtmp", [64, 192], F32)
    W_sb = sbuf("W_sb", [64, NH * MF], BF16)
    sincos = sbuf("sincos", [64, CH], BF16)
    sincos_sw = sbuf("sincos_sw", [64, CH], BF16)
    sc_f32 = sbuf("sc_f32", [64, 2 * CH], F32)
    rwT = sbuf("rwT", [128, 64], BF16)
    rw_nat = sbuf("rw_nat", [8, E], F32)
    b1T = sbuf("b1T", [128, 8 * 32], F32)
    b1tmp = sbuf("b1tmp", [32, 128], F32)
    b2t = sbuf("b2t", [1, E], BF16)
    ones1 = sbuf("ones1", [1, 128], BF16)
    halfc = sbuf("halfc", [64, 1], BF16)
    hi_t = sbuf("hi_t", [1, CH], BF16)
    lo_t = sbuf("lo_t", [1, CH], BF16)
    nsqf = sbuf("nsqf", [1, CH], F32)
    stat = sbuf("stat", [128, 16], F32)
    nsqneg = sbuf("nsqneg", [128, 64], F32)
    dcol = sbuf("dcol", [128, 2], F32)
    S_sb = sbuf("S_sb", [128, 32 * 65], F32)
    S_bf = sbuf("S_bf", [128, 32 * 65], BF16)
    gates = sbuf("gates", [128, 8 * NT128], F32)
    gtmp = sbuf("gtmp", [128, 8], F32)
    gcol = sbuf("gcol", [128, 4], F32)

    abf_t = sbuf("arena_bf", [128, 61440], BF16)
    af_t = sbuf("arena_f32", [128, 10240], F32)
    ABF = Arena(abf_t, 61440)
    AFA = Arena(af_t, 10240)

    ps_ctxs = []
    P = []
    for i in range(8):
        ctx = nc.psum_tensor(f"P{i}", [128, 512], F32)
        P.append(ctx.__enter__())
        ps_ctxs.append(ctx)

    B.init_consts(cpool)

    with nc.Block() as block:
        ev_eps = B.const(LN_EPS)
        e_id = B.dma("g", out=id128[:, :], in_=id64_ap)
        e_h64 = B.dma("g", out=h64[:, :], in_=h64_ap)
        e_hp = B.dma("g", out=hp[:, :], in_=hp_ap)
        e_rw = B.dma("g", out=rw_nat[:, :], in_=rw_t.ap())
        ev_ones = B.op("v", "memset", ones1[:, :], 1.0)
        ev_half = B.op("v", "memset", halfc[:, :], 0.5)

        e_wqk0 = B.dma("g",
                       out=dram_ap(wqkv_bf, 0,
                                   [(64 * E, 32), (E, 32), (1, E)]),
                       in_=dram_ap(wqkv_t, 0,
                                   [(64 * E, 32), (2 * E, 32), (1, E)]))
        e_wqk1 = B.dma("g",
                       out=dram_ap(wqkv_bf, 32 * E,
                                   [(64 * E, 32), (E, 32), (1, E)]),
                       in_=dram_ap(wqkv_t, E,
                                   [(64 * E, 32), (2 * E, 32), (1, E)]))
        e_wqk = [e_wqk0, e_wqk1]
        e_wv = B.dma("g", out=wqkv_bf.ap()[2048:3072, :],
                     in_=wqkv_t.ap()[2048:3072, :])
        e_wout = B.dma("g", out=wout_bf.ap()[:, :], in_=wout_t.ap()[:, :])
        e_b2c = B.dma("g", out=b2_bf.ap()[:, :], in_=b2_t.ap()[:, :])
        e_w1c, e_w2c = [], []
        for e in range(8):
            e_w1c.append(B.dma("g", out=w1_bf.ap()[e], in_=w1_t.ap()[e]))
            e_w2c.append(B.dma("g", out=w2_bf.ap()[e], in_=w2_t.ap()[e]))

        def sign_half(row0, ev_load, pslot):
            eT = B.op("t", "transpose", P[pslot][0:64, 0:96], sctmp[0:96, 0:64],
                      id128[0:96, 0:96], deps=[ev_load, e_id])
            eN = B.op("v", "tensor_copy", scN[:, row0:row0 + 96],
                      P[pslot][0:64, 0:96], deps=[eT])
            eP0 = B.op("v", "tensor_copy",
                       sperm[0:96, 0:64].rearrange("r (par j) -> r par j", par=2),
                       sctmp[0:96, 0:64].rearrange("r (j par) -> r par j", par=2),
                       deps=[ev_load])
            eT2 = B.op("t", "transpose", P[pslot][0:64, 96:192], sperm[0:96, 0:64],
                       id128[0:96, 0:96], deps=[eP0, e_id])
            eP = B.op("v", "tensor_copy", scP[:, row0:row0 + 96],
                      P[pslot][0:64, 96:192], deps=[eT2])
            return eN, eP

        e_sgA = B.dma("g", out=sctmp[0:96, :],
                      in_=signs_t.ap().rearrange("h r p d -> (h r p) d")[0:96, :])
        eN_a, eP_a = sign_half(0, e_sgA, 7)
        e_sgB = B.dma("g", out=sctmp[0:96, :],
                      in_=signs_t.ap().rearrange("h r p d -> (h r p) d")[96:192, :],
                      deps=[eN_a, eP_a])
        eN_b, eP_b = sign_half(96, e_sgB, 7)
        e_sc = B.dma("g", out=sctmp[0:64, :],
                     in_=scales_t.ap().rearrange("h r d -> (h r) d"),
                     deps=[eN_b, eP_b])
        eTsc = B.op("t", "transpose", P[7][0:64, 192:256], sctmp[0:64, 0:64],
                    id64[:, :], deps=[e_sc, e_id])
        e_gc = B.op("v", "tensor_copy", gcols[:, :], P[7][0:64, 192:256], deps=[eTsc])

        Z1 = wtmp[:, 0:64]
        Z2 = wtmp[:, 64:128]
        Z3 = wtmp[:, 128:192]
        prevW = None
        for h in range(NH):
            for r in range(4):
                idx = h * 4 + r
                s1c = scP[:, idx * 3 + 0:idx * 3 + 1]
                s2c = scN[:, idx * 3 + 1:idx * 3 + 2]
                s3c = scN[:, idx * 3 + 2:idx * 3 + 3]
                gc = gcols[:, idx:idx + 1]
                e1 = B.op("v", "tensor_scalar", Z1, h64[:, :], gc, None,
                          op0=ALU.mult, deps=[e_gc, e_h64, prevW])
                e2 = B.op("t", "transpose", P[6][0:64, 0:64], Z1, id64[:, :],
                          deps=[e1, e_id])
                e3 = B.op("v", "tensor_scalar", Z3, P[6][0:64, 0:64], s3c, None,
                          op0=ALU.mult, deps=[e2, eN_b])
                e4 = B.op("v", "tensor_scalar", Z2, h64[:, :], s2c, None,
                          op0=ALU.mult, deps=[e3])
                e5 = B.op("t", "transpose", P[6][0:64, 64:128], Z2, id64[:, :],
                          deps=[e4])
                e5c = B.op("v", "tensor_copy", Z2, P[6][0:64, 64:128], deps=[e5])
                e6 = B.op("t", "matmul", P[6][0:64, 128:192], Z2, Z3,
                          start=True, stop=True, deps=[e5c, e3])
                e6c = B.op("v", "tensor_copy", Z3, P[6][0:64, 128:192], deps=[e6])
                e7 = B.op("v", "tensor_scalar", Z1, hp[:, :], s1c, None,
                          op0=ALU.mult, deps=[eP_b, e_hp, e6])
                e8 = B.op("t", "transpose", P[6][0:64, 192:256], Z1, id64[:, :],
                          deps=[e7])
                e8c = B.op("v", "tensor_copy", Z1, P[6][0:64, 192:256], deps=[e8])
                e9 = B.op("t", "matmul", P[6][0:64, 256:320], Z1, Z3,
                          start=True, stop=True, deps=[e8c, e6c])
                prevW = B.op("v", "tensor_scalar",
                             W_sb[:, h * MF + r * 64: h * MF + (r + 1) * 64],
                             P[6][0:64, 256:320], 1.0 / 4096.0, None,
                             op0=ALU.mult, deps=[e9])
        ev_W = prevW
        if dbg is True:
            B.dma("y", out=dbg_aps["d_W"], in_=W_sb[:, :], deps=[ev_W])

        prev_rw = None
        for hc in range(8):
            eT = B.op("t", "transpose", P[7][:, 8:16],
                      rw_nat[0:8, hc * 128:(hc + 1) * 128], id64[0:8, 0:8],
                      deps=[e_rw, e_id, prev_rw, e_gc])
            prev_rw = B.op("v", "tensor_copy", rwT[:, hc * 8:(hc + 1) * 8],
                           P[7][:, 8:16], deps=[eT])
        ev_rwT = prev_rw
        prev_b1 = None
        for e in range(8):
            eL = B.dma("g", out=b1tmp[0:32, :],
                       in_=b1_t.ap()[e].rearrange("(fc p) -> fc p", p=128),
                       deps=[prev_b1])
            eT = B.op("t", "transpose", P[7][:, 16:48], b1tmp[0:32, 0:128],
                      id64[0:32, 0:32], deps=[eL, e_id, ev_rwT, prev_b1])
            prev_b1 = B.op("v", "tensor_copy", b1T[:, e * 32:(e + 1) * 32],
                           P[7][:, 16:48], deps=[eT])
        ev_b1T = prev_b1

        stg = [ABF.alloc(2048), ABF.alloc(2048)]
        stg2 = [ABF.alloc(2048), ABF.alloc(2048)]
        stg_free = [None, None]
        stg2_free = [None, None]
        si = 0
        ev_w1T = {}
        ev_w2T = {}
        def prep_tile(job):
            nonlocal si
            in_ap, out_ap, dep_load, e, kind = job
            st = stg[si % 2]
            st2 = stg2[si % 2]
            eT = B.dma("y", out=st[:, 0:1024], in_=in_ap, transpose=True,
                       deps=[dep_load, stg_free[si % 2]])
            eC = B.op("v", "tensor_copy", st2[:, 0:1024], st[:, 0:1024],
                      deps=[eT, stg2_free[si % 2]])
            stg_free[si % 2] = eC
            eW = B.dma("g", out=out_ap, in_=st2[:, 0:1024], deps=[eC])
            stg2_free[si % 2] = eW
            si += 1
            if kind == 1:
                ev_w1T[e] = eW
            else:
                ev_w2T[e] = eW

        prep_jobs = []
        for e in range(8):
            for hc in range(8):
                for q in range(4):
                    prep_jobs.append((
                        w1_bf.ap()[e][q * 1024:(q + 1) * 1024,
                                      hc * 128:(hc + 1) * 128],
                        w1T_sc.ap()[e][hc][q], e_w1c[e], e, 1))
        for e in range(8):
            for fc in range(32):
                prep_jobs.append((
                    w2_bf.ap()[e][:, fc * 128:(fc + 1) * 128],
                    w2T_sc.ap()[e][fc], e_w2c[e], e, 2))
        prep_jobs.reverse()

        def run_prep_jobs(n):
            for _ in range(n):
                if prep_jobs:
                    prep_tile(prep_jobs.pop())
        ABF.mark("attn")
        AFA.mark("attn")

        xln_t = ABF.alloc(4 * E)
        xlnT_t = [ABF.alloc(8 * CH), ABF.alloc(8 * CH)]
        wbig_t = ABF.alloc(8 * 2048)
        kq_t = ABF.alloc(NH * CH)
        ksq_t = ABF.alloc(2 * CH)
        _u1 = ABF.alloc(max(NH * MF, 4 * E))
        kphi_t = _u1[:, 0:NH * MF]
        cat_t = _u1[:, 0:4 * E]
        _u2 = ABF.alloc(max(4 * NH * 65, 4 * E))
        v_t = _u2[:, 0:4 * NH * 65]
        x2ln_t = _u2[:, 0:4 * E]
        qphi_t = ABF.alloc(2 * 1024)
        catT_t = ABF.alloc(8 * CH)
        x_t = AFA.alloc(4 * E)
        out1_t = AFA.alloc(4 * E)

        def gen_sincos(c, deps_extra):
            e1 = B.dma("g", out=sincos[:, :],
                       in_=cossin_ap[:, c * CH:(c + 1) * CH], deps=deps_extra)
            e2 = B.dma("g", out=sincos_sw[:, :],
                       in_=cossin_sw_ap[:, c * CH:(c + 1) * CH], deps=deps_extra)
            return [e1, e2]

        def layer_norm(src_f32, dst_bf16, deps):
            s0 = B.op("v", "bn_stats", stat[:, 0:6], src_f32[:, 0:512], deps=deps)
            s1 = B.op("v", "bn_stats", stat[:, 6:12], src_f32[:, 512:1024], deps=deps)
            s2 = B.op("v", "bn_aggr", stat[:, 12:14], stat[:, 0:12], deps=[s0, s1])
            s3 = B.op("s", "activation", stat[:, 14:15], stat[:, 13:14], AF.Sqrt,
                      bias=LN_EPS, deps=[s2, ev_eps])
            s4 = B.op("v", "reciprocal", stat[:, 14:15], stat[:, 14:15], deps=[s3])
            return B.op("v", "tensor_scalar", dst_bf16, src_f32,
                        stat[:, 12:13], stat[:, 14:15],
                        op0=ALU.subtract, op1=ALU.mult, deps=[s4])

        ev_x_use = None
        ev_xln_use = None
        ev_xlnT_use = [None, None]
        ev_kq_use = None
        ev_ksq_use = [None, None]
        ev_kphi_use = None
        ev_v_use = None
        ev_ps = [None] * 8
        ev_rope_prev = {"v": None, "g": None}
        ev_ps[6] = ev_W
        ev_ps[7] = ev_b1T
        ev_nsq_evac = ev_W
        ev_sincos_use = None

        ev_qphi_use = [None, None]
        ev_S0 = B.op("v", "memset", S_sb[:, :], 0.0)
        vz = B.op("v", "memset", v_t[:, :], 0.0)
        one_evs = []
        for tt in range(4):
            for h in range(NH):
                col = tt * NH * 65 + h * 65 + 64
                one_evs.append(B.op("v", "memset", v_t[:, col:col + 1], 1.0,
                                    deps=[vz]))
        ev_v_use = one_evs[-1]

        ev_wbigK = []
        for hc in range(8):
            ev = B.dma("y", out=wbig_t[:, hc * 2048:(hc + 1) * 2048],
                       in_=wqkv_bf.ap()[1024:3072, hc * 128:(hc + 1) * 128],
                       transpose=True,
                       deps=[e_wqk, e_wv, e_wout, e_b2c, e_w1c[0], e_w1c[1]])
            ev_wbigK.append(ev)

        for c in range(NCH):
            rows = slice(c * CH, (c + 1) * CH)
            e_x = B.dma("y", out=x_t.rearrange("p (a h) -> p a h", a=4),
                        in_=x_ap[rows, :].rearrange("(a p) h -> p a h", p=128),
                        deps=[ev_x_use])
            ev_sc = gen_sincos(c, [ev_sincos_use, ev_ps[7]])
            ln_evs = []
            for a in range(4):
                ln_evs.append(layer_norm(x_t[:, a * E:(a + 1) * E],
                                         xln_t[:, a * E:(a + 1) * E],
                                         [e_x, ev_xln_use]))
            ev_x_use = ln_evs[-1]
            e_wx = B.dma("g",
                         out=xln_sc.ap()[rows, :].rearrange("(a p) h -> p a h", p=128),
                         in_=xln_t.rearrange("p (a h) -> p a h", a=4),
                         deps=ln_evs)
            ev_xln_use = e_wx
            if dbg is True:
                _ed = B.dma("y", out=dbg_aps["d_xln"][rows, :].rearrange("(a p) h -> p a h", p=128),
                      in_=xln_t.rearrange("p (a h) -> p a h", a=4), deps=ln_evs)
                B.fence("v", [_ed])

            buf = c % 2
            exT = []
            for hc in range(8):
                ev = B.dma("y", out=xlnT_t[buf][:, hc * CH:(hc + 1) * CH],
                           in_=xln_sc.ap()[rows, hc * 128:(hc + 1) * 128],
                           transpose=True, deps=[e_wx, ev_xlnT_use[buf]])
                exT.append(ev)
            xlnT = xlnT_t[buf]
            B.fence("t", exT + ev_wbigK + [ev_W])

            kq_evs = {}
            for ic in range(8):
                pb = ic % 2
                B.fence("t", [ev_ps[pb]])
                mm = None
                for hc in range(8):
                    mm = B.op("t", "matmul", P[pb][:, 0:CH],
                              wbig_t[:, hc * 2048 + ic * 128: hc * 2048 + (ic + 1) * 128],
                              xlnT[:, hc * CH:(hc + 1) * CH],
                              start=(hc == 0), stop=(hc == 7), inc=(hc == 7))
                for half in range(2):
                    h = 2 * ic + half
                    ev = B.op("s", "copy", kq_t[0:64, h * CH:(h + 1) * CH],
                              P[pb][64 * half:64 * (half + 1), 0:CH],
                              deps=[mm, ev_kq_use])
                    kq_evs[h] = ev
                ev_ps[pb] = kq_evs[2 * ic + 1]

            v_evs = [None] * 4
            for tt in range(4):
                for half in range(2):
                    pb = tt % 2
                    B.fence("t", [ev_ps[pb]])
                    mm = None
                    for hc in range(8):
                        mm = B.op("t", "matmul", P[pb][:, 0:CH],
                                  xlnT[:, hc * CH + tt * 128: hc * CH + (tt + 1) * 128],
                                  wbig_t[:, hc * 2048 + 1024 + half * CH:
                                         hc * 2048 + 1024 + (half + 1) * CH],
                                  start=(hc == 0), stop=(hc == 7), inc=(hc == 7))
                    base = tt * NH * 65 + half * 8 * 65
                    ev = B.op("s", "copy",
                              v_t[:, base:base + 8 * 65].rearrange(
                                  "p (h d) -> p h d", d=65)[:, :, 0:64],
                              P[pb][:, 0:CH].rearrange("p (h d) -> p h d", d=64),
                              deps=[mm, ev_v_use])
                    ev_ps[pb] = ev
                    v_evs[tt] = ev

            rope_map = {}
            B.fence("v", [ev_sc])
            B.fence("g", [ev_sc, ev_qphi_use[0], ev_qphi_use[1]])
            prev_rope = ev_rope_prev
            for h in range(NH):
                hs = slice(h * CH, (h + 1) * CH)
                evh = kq_evs[h]
                en = "v" if h % 2 == 0 else "g"
                if en == "v":
                    q1 = sc_f32[0:32, 0:CH]
                    q2 = sc_f32[0:32, CH:2 * CH]
                    q3 = sc_f32[32:64, CH:2 * CH]
                    q4 = sc_f32[32:64, 0:CH]
                else:
                    q1 = qphi_t[0:32, 0:CH]
                    q2 = qphi_t[0:32, CH:2 * CH]
                    q3 = qphi_t[32:64, CH:2 * CH]
                    q4 = qphi_t[32:64, 0:CH]
                t1 = B.op(en, "tensor_tensor", q1, kq_t[0:32, hs],
                          sincos[0:32, :], op=ALU.mult,
                          deps=[evh, prev_rope[en]])
                t2 = B.op(en, "tensor_tensor", q2, kq_t[32:64, hs],
                          sincos[32:64, :], op=ALU.mult, deps=[evh])
                t3 = B.op(en, "tensor_tensor", q3, kq_t[0:32, hs],
                          sincos_sw[0:32, :], op=ALU.mult, deps=[evh])
                t4 = B.op(en, "tensor_tensor", q4, kq_t[32:64, hs],
                          sincos_sw[32:64, :], op=ALU.mult, deps=[evh])
                o1 = B.op(en, "tensor_tensor", kq_t[0:32, hs], q1, q2,
                          op=ALU.subtract, deps=[t4])
                o2 = B.op(en, "tensor_tensor", kq_t[32:64, hs], q3, q4,
                          op=ALU.add, deps=[t4])
                rope_map[h] = (o1, o2)
                prev_rope[en] = o2
            ev_sincos_use = [rope_map[NH - 2][1], rope_map[NH - 1][1]]
            if dbg is True and c == 0:
                _ed = B.dma("y", out=dbg_aps["d_kq"], in_=kq_t[0:64, :],
                      deps=[e for pair in rope_map.values() for e in pair])
                B.fence("s", [_ed])

            nsq_mm_last = None
            sq_map = {}
            for h in range(NH):
                o1, o2 = rope_map[h]
                sqb = h % 2
                esq = B.op("s", "activation", ksq_t[0:64, sqb * CH:(sqb + 1) * CH],
                           kq_t[0:64, h * CH:(h + 1) * CH], AF.Square,
                           deps=[o1, o2, ev_ksq_use[sqb]])
                B.fence("t", [esq, ev_half, ev_nsq_evac])
                for tt in range(4):
                    col = h * 4 + tt
                    nsq_mm_last = B.op("t", "matmul", P[6][:, col:col + 1],
                                       ksq_t[0:64, sqb * CH + tt * 128:
                                             sqb * CH + (tt + 1) * 128],
                                       halfc[:, :], start=True, stop=True,
                                       inc=(tt == 3))
                sq_map[h] = nsq_mm_last
                ev_ksq_use[sqb] = nsq_mm_last
            e_nsqe = B.op("v", "tensor_scalar", nsqneg[:, :], P[6][:, 0:64],
                          -1.0, None, op0=ALU.mult, deps=[nsq_mm_last])
            ev_nsq_evac = e_nsqe

            for tt in range(4):
                phi_evs = {}
                for h in range(NH):
                    pb = 2 + (h % 2)
                    o1, o2 = rope_map[h]
                    B.fence("t", [ev_ps[pb]])
                    mmp = B.op("t", "matmul", P[pb][:, 0:MF],
                               kq_t[0:64, h * CH + tt * 128: h * CH + (tt + 1) * 128],
                               W_sb[:, h * MF:(h + 1) * MF],
                               start=True, stop=True, deps=[o1, o2])
                    eex = B.op("s", "activation", kphi_t[:, h * MF:(h + 1) * MF],
                               P[pb][:, 0:MF], AF.Exp,
                               bias=nsqneg[:, h * 4 + tt: h * 4 + tt + 1],
                               deps=[mmp, e_nsqe, ev_kphi_use])
                    ev_ps[pb] = eex
                    phi_evs[h] = eex
                if dbg is True and c == 0 and tt == 0:
                    _ed = B.dma("y", out=dbg_aps["d_kphi"], in_=kphi_t[:, :],
                          deps=list(phi_evs.values()))
                    B.fence("s", [_ed])
                groups = [(h, mc) for h in range(NH) for mc in range(2)]
                s_last = None
                for gi in range(0, 32, 14):
                    wave = groups[gi:gi + 14]
                    nbank = [min(7, len(wave)), max(0, len(wave) - 7)]
                    bank_ev = [None, None]
                    for bi in range(2):
                        if nbank[bi] == 0:
                            continue
                        bank = 4 + bi
                        B.fence("t", [v_evs[tt], ev_ps[bank]] +
                                [phi_evs[h] for (h, mc) in wave])
                        for wj in range(nbank[bi]):
                            h, mc = wave[bi * 7 + wj]
                            bank_ev[bi] = B.op(
                                "t", "matmul", P[bank][:, wj * 65:(wj + 1) * 65],
                                kphi_t[:, h * MF + mc * 128: h * MF + (mc + 1) * 128],
                                v_t[:, tt * NH * 65 + h * 65:
                                    tt * NH * 65 + (h + 1) * 65],
                                start=True, stop=True, inc=(wj == nbank[bi] - 1))
                    for bi in range(2):
                        for wj in range(nbank[bi]):
                            h, mc = wave[bi * 7 + wj]
                            bank = 4 + bi
                            scol = (h * 2 + mc) * 65
                            s_last = B.op("v", "tensor_tensor",
                                          S_sb[:, scol:scol + 65],
                                          S_sb[:, scol:scol + 65],
                                          P[bank][:, wj * 65:(wj + 1) * 65],
                                          op=ALU.add, deps=[bank_ev[bi], ev_S0])
                        ev_ps[4 + bi] = s_last
                ev_kphi_use = s_last
                ev_v_use = s_last
            ev_xlnT_use[buf] = B.now("t")
            run_prep_jobs(22 if c > 0 else 0)

        run_prep_jobs(len(prep_jobs))
        ev_S_done = ev_kphi_use
        if dbg is True:
            _ed = B.dma("y", out=dbg_aps["d_S"], in_=S_sb[:, :], deps=[ev_S_done])
            B.fence("v", [_ed])

        e_Sbf = B.op("v", "tensor_copy", S_bf[:, :], S_sb[:, :], deps=[ev_S_done])

        ev_wbigQ = []
        for hc in range(8):
            ev = B.dma("y", out=wbig_t[:, hc * 2048: hc * 2048 + 1024],
                       in_=wqkv_bf.ap()[0:1024, hc * 128:(hc + 1) * 128],
                       transpose=True, deps=[e_wqk, ev_xlnT_use[0], ev_xlnT_use[1],
                                             B.now("t")])
            ev_wbigQ.append(ev)
            ev = B.dma("y", out=wbig_t[:, hc * 2048 + 1024:(hc + 1) * 2048],
                       in_=wout_bf.ap()[0:1024, hc * 128:(hc + 1) * 128],
                       transpose=True, deps=[e_wout])
            ev_wbigQ.append(ev)

        ev_cat_use = None
        ev_catT_use = None
        ev_out1_use = None
        ev_x2ln_use = None
        ev_hilo_use = None

        for c in range(NCH):
            rows = slice(c * CH, (c + 1) * CH)
            e_x = B.dma("y", out=x_t.rearrange("p (a h) -> p a h", a=4),
                        in_=x_ap[rows, :].rearrange("(a p) h -> p a h", p=128),
                        deps=[ev_x_use])
            ev_sc = gen_sincos(c, [ev_sincos_use, ev_ps[7]])
            buf = c % 2
            exT = []
            for hc in range(8):
                ev = B.dma("y", out=xlnT_t[buf][:, hc * CH:(hc + 1) * CH],
                           in_=xln_sc.ap()[rows, hc * 128:(hc + 1) * 128],
                           transpose=True, deps=[ev_xln_use, ev_xlnT_use[buf]])
                exT.append(ev)
            xlnT = xlnT_t[buf]
            B.fence("t", exT + ev_wbigQ + [e_Sbf])

            kq_evs = {}
            for ic in range(8):
                pb = ic % 2
                B.fence("t", [ev_ps[pb]])
                mm = None
                for hc in range(8):
                    mm = B.op("t", "matmul", P[pb][:, 0:CH],
                              wbig_t[:, hc * 2048 + ic * 128: hc * 2048 + (ic + 1) * 128],
                              xlnT[:, hc * CH:(hc + 1) * CH],
                              start=(hc == 0), stop=(hc == 7), inc=(hc == 7))
                for half in range(2):
                    h = 2 * ic + half
                    kq_evs[h] = B.op("s", "copy", kq_t[0:64, h * CH:(h + 1) * CH],
                                     P[pb][64 * half:64 * (half + 1), 0:CH],
                                     deps=[mm, ev_kq_use])
                ev_ps[pb] = kq_evs[2 * ic + 1]
            rope_map = {}
            B.fence("v", [ev_sc])
            B.fence("g", [ev_sc, ev_qphi_use[0], ev_qphi_use[1]])
            prev_rope = ev_rope_prev
            for h in range(NH):
                hs = slice(h * CH, (h + 1) * CH)
                evh = kq_evs[h]
                en = "v" if h % 2 == 0 else "g"
                if en == "v":
                    q1 = sc_f32[0:32, 0:CH]
                    q2 = sc_f32[0:32, CH:2 * CH]
                    q3 = sc_f32[32:64, CH:2 * CH]
                    q4 = sc_f32[32:64, 0:CH]
                else:
                    q1 = qphi_t[0:32, 0:CH]
                    q2 = qphi_t[0:32, CH:2 * CH]
                    q3 = qphi_t[32:64, CH:2 * CH]
                    q4 = qphi_t[32:64, 0:CH]
                t1 = B.op(en, "tensor_tensor", q1, kq_t[0:32, hs],
                          sincos[0:32, :], op=ALU.mult,
                          deps=[evh, prev_rope[en]])
                t2 = B.op(en, "tensor_tensor", q2, kq_t[32:64, hs],
                          sincos[32:64, :], op=ALU.mult, deps=[evh])
                t3 = B.op(en, "tensor_tensor", q3, kq_t[0:32, hs],
                          sincos_sw[0:32, :], op=ALU.mult, deps=[evh])
                t4 = B.op(en, "tensor_tensor", q4, kq_t[32:64, hs],
                          sincos_sw[32:64, :], op=ALU.mult, deps=[evh])
                o1 = B.op(en, "tensor_tensor", kq_t[0:32, hs], q1, q2,
                          op=ALU.subtract, deps=[t4])
                o2 = B.op(en, "tensor_tensor", kq_t[32:64, hs], q3, q4,
                          op=ALU.add, deps=[t4])
                rope_map[h] = (o1, o2)
                prev_rope[en] = o2
            ev_sincos_use = [rope_map[NH - 2][1], rope_map[NH - 1][1]]

            cat_evs = []
            for h in range(NH):
                o1, o2 = rope_map[h]
                sqb = h % 2
                esq = B.op("s", "activation", ksq_t[0:64, sqb * CH:(sqb + 1) * CH],
                           kq_t[0:64, h * CH:(h + 1) * CH], AF.Square,
                           deps=[o1, o2, ev_ksq_use[sqb]])
                nb = 4 + (h % 2)
                B.fence("t", [esq, ev_half, ev_ps[nb]])
                mmn = B.op("t", "matmul", P[nb][0:1, 0:CH], halfc[:, :],
                           ksq_t[0:64, sqb * CH:(sqb + 1) * CH],
                           start=True, stop=True)
                ev_ksq_use[sqb] = mmn
                en1 = B.op("v", "tensor_scalar", nsqf[:, :], P[nb][0:1, 0:CH],
                           -1.0, None, op0=ALU.mult, deps=[mmn, ev_hilo_use])
                en2 = B.op("v", "tensor_copy", hi_t[:, :], nsqf[:, :], deps=[en1])
                en3 = B.op("v", "tensor_tensor", lo_t[:, :], nsqf[:, :], hi_t[:, :],
                           op=ALU.subtract, deps=[en2])
                ev_ps[nb] = en1
                qb = h % 2
                phi_evs = []
                for mc in range(2):
                    pb = 2 + mc
                    B.fence("t", [ev_ps[pb], en3, ev_ones])
                    m1 = B.op("t", "matmul", P[pb][:, 0:CH],
                              W_sb[:, h * MF + mc * 128: h * MF + (mc + 1) * 128],
                              kq_t[0:64, h * CH:(h + 1) * CH],
                              start=True, stop=False, inc=False)
                    m2 = B.op("t", "matmul", P[pb][:, 0:CH], ones1[:, :],
                              hi_t[:, :], start=False, stop=False, inc=False)
                    m3 = B.op("t", "matmul", P[pb][:, 0:CH], ones1[:, :],
                              lo_t[:, :], start=False, stop=True, inc=True)
                    eex = B.op("s", "activation",
                               qphi_t[:, qb * 1024 + mc * CH: qb * 1024 + (mc + 1) * CH],
                               P[pb][:, 0:CH], AF.Exp,
                               deps=[m3, ev_qphi_use[qb], prev_rope["g"],
                                     prev_rope["v"]])
                    ev_ps[pb] = eex
                    phi_evs.append(eex)
                ev_hilo_use = phi_evs[-1]
                num_last = None
                for tt in range(4):
                    nb2 = 6 + (tt % 2)
                    B.fence("t", phi_evs + [e_Sbf, ev_ps[nb2]])
                    mm = None
                    for mc in range(2):
                        mm = B.op("t", "matmul", P[nb2][:, 0:65],
                                  qphi_t[:, qb * 1024 + mc * CH + tt * 128:
                                         qb * 1024 + mc * CH + (tt + 1) * 128],
                                  S_bf[:, (h * 2 + mc) * 65:(h * 2 + mc + 1) * 65],
                                  start=(mc == 0), stop=(mc == 1), inc=(mc == 1))
                    d1 = B.op("v", "tensor_scalar", dcol[:, 0:1], P[nb2][:, 64:65],
                              EPS_DENOM, None, op0=ALU.max, deps=[mm])
                    d2 = B.op("v", "reciprocal", dcol[:, 1:2], dcol[:, 0:1], deps=[d1])
                    d3 = B.op("v", "tensor_scalar",
                              cat_t[:, tt * E + h * 64: tt * E + (h + 1) * 64],
                              P[nb2][:, 0:64], dcol[:, 1:2], None,
                              op0=ALU.mult, deps=[d2, ev_cat_use])
                    ev_ps[nb2] = d3
                    num_last = d3
                    cat_evs.append(d3)
                ev_qphi_use[qb] = num_last
            ev_kq_use = cat_evs[-1]

            e_wc = B.dma("g",
                         out=cat_sc.ap()[rows, :].rearrange("(a p) h -> p a h", p=128),
                         in_=cat_t.rearrange("p (a h) -> p a h", a=4),
                         deps=cat_evs)
            ev_cat_use = e_wc
            if dbg is True:
                _ed = B.dma("y", out=dbg_aps["d_cat"][rows, :].rearrange("(a p) h -> p a h", p=128),
                      in_=cat_t.rearrange("p (a h) -> p a h", a=4), deps=cat_evs)
                B.fence("v", [_ed])
            ecT = []
            for hc in range(8):
                ev = B.dma("y", out=catT_t[:, hc * CH:(hc + 1) * CH],
                           in_=cat_sc.ap()[rows, hc * 128:(hc + 1) * 128],
                           transpose=True, deps=[e_wc, ev_catT_use])
                ecT.append(ev)
            B.fence("t", ecT)
            op_evs = []
            ln2_evs = []
            for tt in range(4):
                for half in range(2):
                    pb = half
                    B.fence("t", [ev_ps[pb]])
                    mm = None
                    for hc in range(8):
                        mm = B.op("t", "matmul", P[pb][:, 0:CH],
                                  catT_t[:, hc * CH + tt * 128: hc * CH + (tt + 1) * 128],
                                  wbig_t[:, hc * 2048 + 1024 + half * CH:
                                         hc * 2048 + 1024 + (half + 1) * CH],
                                  start=(hc == 0), stop=(hc == 7), inc=(hc == 7))
                    ev = B.op("v", "tensor_tensor",
                              out1_t[:, tt * E + half * CH: tt * E + (half + 1) * CH],
                              P[pb][:, 0:CH],
                              x_t[:, tt * E + half * CH: tt * E + (half + 1) * CH],
                              op=ALU.add, deps=[mm, e_x, ev_out1_use])
                    ev_ps[pb] = ev
                    op_evs.append(ev)
                ln2_evs.append(layer_norm(out1_t[:, tt * E:(tt + 1) * E],
                                          x2ln_t[:, tt * E:(tt + 1) * E],
                                          [op_evs[-1], op_evs[-2], ev_x2ln_use]))
            ev_x_use = op_evs[-1]
            ev_catT_use = B.now("t")
            e_wo1 = B.dma("g",
                          out=out1_sc.ap()[rows, :].rearrange("(a p) h -> p a h", p=128),
                          in_=out1_t.rearrange("p (a h) -> p a h", a=4),
                          deps=op_evs)
            ev_out1_use = e_wo1
            e_wx2 = B.dma("g",
                          out=x2ln_sc.ap()[rows, :].rearrange("(a p) h -> p a h", p=128),
                          in_=x2ln_t.rearrange("p (a h) -> p a h", a=4),
                          deps=ln2_evs)
            ev_x2ln_use = e_wx2
            if dbg == "light":
                e_dx2 = B.dma("y", out=dbg_aps["d_x2A"][rows, :].rearrange(
                    "(a p) h -> p a h", p=128),
                    in_=x2ln_t.rearrange("p (a h) -> p a h", a=4), deps=ln2_evs)
                ev_x2ln_use = [e_wx2, e_dx2]
            if dbg is True:
                _ed = B.dma("y", out=dbg_aps["d_out1"][rows, :].rearrange("(a p) h -> p a h", p=128),
                      in_=out1_t.rearrange("p (a h) -> p a h", a=4), deps=op_evs)
                B.fence("v", [_ed])
                _ed = B.dma("y", out=dbg_aps["d_x2ln"][rows, :].rearrange("(a p) h -> p a h", p=128),
                      in_=x2ln_t.rearrange("p (a h) -> p a h", a=4), deps=ln2_evs)
                B.fence("v", [_ed])

        for en in ENGINES:
            B.fence(en, B.everything())
        ABF.reset("attn")
        AFA.reset("attn")
        x2lnT_t = ABF.alloc(8 * TB)
        h1_t2 = [ABF.alloc(8 * 512) for _ in range(TB // 512)]
        w1s_t = [ABF.alloc(8 * 1024), ABF.alloc(8 * 1024)]
        w2s_t = [ABF.alloc(8 * 1024), ABF.alloc(8 * 1024)]
        oacc_t = AFA.alloc(NT128 * E)
        o1b_t = AFA.alloc(2 * E)

        ev_x2lnT_use = None
        ev_oacc_use = None
        ev_w1s_use = [None, None]
        ev_w2s_use = [None, None]
        ev_o1b_use = [None, None]
        ev_b2t_use = None
        ev_gates_use = None
        ev_h1_use = {t5: None for t5 in range(TB // 512)}
        ev_osl = {}
        ev_out_dma = []

        for b in range(NBLK):
            rows = slice(b * TB, (b + 1) * TB)
            exT = []
            for hc in range(8):
                ev = B.dma("y", out=x2lnT_t[:, hc * TB:(hc + 1) * TB],
                           in_=x2ln_sc.ap()[rows, hc * 128:(hc + 1) * 128],
                           transpose=True, deps=[ev_x2ln_use, ev_x2lnT_use])
                exT.append(ev)
            B.fence("t", exT + [ev_rwT])

            gate_evs = []
            for tt in range(NT128):
                pb = 6 + (tt % 2)
                B.fence("t", [ev_ps[pb]])
                mm = None
                for hc in range(8):
                    mm = B.op("t", "matmul", P[pb][:, 0:8],
                              x2lnT_t[:, hc * TB + tt * 128: hc * TB + (tt + 1) * 128],
                              rwT[:, hc * 8:(hc + 1) * 8],
                              start=(hc == 0), stop=(hc == 7), inc=(hc == 7))
                gsl = gates[:, tt * 8:(tt + 1) * 8]
                g1 = B.op("v", "reduce_max", gcol[:, 0:1], P[pb][:, 0:8], axis=AX.X,
                          deps=[mm, ev_gates_use])
                g2 = B.op("v", "tensor_scalar", gtmp[:, :], P[pb][:, 0:8],
                          gcol[:, 0:1], None, op0=ALU.is_equal, deps=[g1])
                g3 = B.op("v", "scalar_tensor_tensor", gtmp[:, :], gtmp[:, :],
                          -1e30, P[pb][:, 0:8], op0=ALU.mult, op1=ALU.add, deps=[g2])
                g4 = B.op("v", "reduce_max", gcol[:, 1:2], gtmp[:, :], axis=AX.X,
                          deps=[g3])
                g5 = B.op("v", "tensor_scalar", gtmp[:, :], P[pb][:, 0:8],
                          gcol[:, 1:2], None, op0=ALU.is_ge, deps=[g4])
                g6 = B.op("v", "tensor_scalar", gsl, P[pb][:, 0:8],
                          gcol[:, 0:1], None, op0=ALU.subtract, deps=[g5])
                g7 = B.op("s", "activation", gsl, gsl, AF.Exp, deps=[g6])
                g8 = B.op("v", "tensor_tensor", gcol[:, 2:3], gcol[:, 1:2],
                          gcol[:, 0:1], op=ALU.subtract, deps=[g4, g7])
                g9 = B.op("s", "activation", gcol[:, 2:3], gcol[:, 2:3], AF.Exp,
                          deps=[g8])
                g10 = B.op("v", "tensor_scalar", gcol[:, 2:3], gcol[:, 2:3],
                           1.0, None, op0=ALU.add, deps=[g9])
                g11 = B.op("v", "reciprocal", gcol[:, 2:3], gcol[:, 2:3], deps=[g10])
                g12 = B.op("v", "tensor_scalar", gsl, gsl, gcol[:, 2:3], None,
                           op0=ALU.mult, deps=[g11])
                g13 = B.op("v", "tensor_tensor", gsl, gsl, gtmp[:, :],
                           op=ALU.mult, deps=[g12])
                ev_ps[pb] = g13
                gate_evs.append(g13)
            ev_oz = B.op("v", "memset", oacc_t[:, :], 0.0,
                         deps=[ev_oacc_use])
            for _k in list(ev_osl):
                ev_osl[_k] = ev_oz
            ev_gates_use = gate_evs[-1]
            if dbg == "light":
                e_dg = B.dma("y", out=dbg_aps["d_gatesA"][b], in_=gates[:, :],
                             deps=gate_evs)
                ev_gates_use = [gate_evs[-1], e_dg]
            if dbg is True:
                _ed = B.dma("y", out=dbg_aps["d_gates"], in_=gates[:, :], deps=gate_evs)
                B.fence("v", [_ed])
            B.fence("t", gate_evs)

            for e in range(8):
                e_b2l = B.dma("g", out=b2t[:, :], in_=b2_bf.ap()[e][None, :],
                              deps=[e_b2c, ev_b2t_use])
                B.fence("t", [e_b2l, ev_b1T, ev_ones])
                NT5 = TB // 512
                for s in range(4):
                    wb = s % 2
                    e_w1l = B.dma("y", out=w1s_t[wb][:, :],
                                  in_=dram_ap(w1T_sc,
                                              (e * 32 + s) * 128 * 1024,
                                              [(1024, 128), (4 * 128 * 1024, 8),
                                               (1, 1024)]),
                                  deps=[ev_w1T[e], ev_w1s_use[wb]])
                    e_w2l = B.dma("y", out=w2s_t[wb][:, :],
                                  in_=w2T_sc.ap()[e][s * 8:(s + 1) * 8].rearrange(
                                      "fc p h -> p fc h"),
                                  deps=[ev_w2T[e], ev_w2s_use[wb]])
                    gelu_evs = {t5: [] for t5 in range(NT5)}
                    for fcl in range(8):
                        pbs = [2 * (fcl % 2) + t5 for t5 in range(NT5)]
                        fdeps = [e_w1l] + [ev_ps[pb] for pb in pbs]
                        if fcl < 2:
                            fdeps += [ev_h1_use[t5] for t5 in range(NT5)]
                        B.fence("t", fdeps)
                        mmf = {}
                        for hc in range(8):
                            for t5 in range(NT5):
                                mmf[t5] = B.op(
                                    "t", "matmul", P[pbs[t5]][:, 0:512],
                                    w1s_t[wb][:, hc * 1024 + fcl * 128:
                                              hc * 1024 + (fcl + 1) * 128],
                                    x2lnT_t[:, hc * TB + t5 * 512:
                                            hc * TB + t5 * 512 + 512],
                                    start=(hc == 0), stop=(hc == 7),
                                    inc=(hc == 7))
                        fc = s * 8 + fcl
                        for t5 in range(NT5):
                            ge = B.op("s", "activation",
                                      h1_t2[t5][:, fcl * 512:(fcl + 1) * 512],
                                      P[pbs[t5]][:, 0:512], AF.Gelu,
                                      bias=b1T[:, e * 32 + fc: e * 32 + fc + 1],
                                      deps=[mmf[t5]])
                            ev_ps[pbs[t5]] = ge
                            gelu_evs[t5].append(ge)
                    for t5 in range(NT5):
                        B.fence("t", gelu_evs[t5] + [e_w2l])
                        stt_last = None
                        for tt in range(4):
                            pb0 = 4 + 2 * (tt % 2)
                            pb1 = pb0 + 1
                            B.fence("t", [ev_ps[pb0], ev_ps[pb1]])
                            mmh = [None, None]
                            for fcl in range(8):
                                last = (s != 0 and fcl == 7)
                                for half in range(2):
                                    mmh[half] = B.op(
                                        "t", "matmul", P[pb0 + half][:, 0:512],
                                        h1_t2[t5][:, fcl * 512 + tt * 128:
                                                  fcl * 512 + (tt + 1) * 128],
                                        w2s_t[wb][:, fcl * 1024 + half * 512:
                                                  fcl * 1024 + (half + 1) * 512],
                                        start=(fcl == 0), stop=last, inc=last)
                            if s == 0:
                                mmh[0] = B.op("t", "matmul", P[pb0][:, 0:512],
                                              ones1[:, :], b2t[:, 0:512],
                                              start=False, stop=True, inc=True)
                                mmh[1] = B.op("t", "matmul", P[pb1][:, 0:512],
                                              ones1[:, :], b2t[:, 512:1024],
                                              start=False, stop=True, inc=True)
                            t128i = t5 * 4 + tt
                            gc_ap = gates[:, t128i * 8 + e: t128i * 8 + e + 1]
                            for half in range(2):
                                okey = (t128i, half)
                                osl = oacc_t[:, t128i * E + half * 512:
                                             t128i * E + (half + 1) * 512]
                                ev = B.op("v", "scalar_tensor_tensor", osl,
                                          P[pb0 + half][:, 0:512], gc_ap, osl,
                                          op0=ALU.mult, op1=ALU.add,
                                          deps=[mmh[half],
                                                ev_osl.get(okey, ev_oz)])
                                ev_osl[okey] = ev
                                ev_ps[pb0 + half] = ev
                                stt_last = ev
                        ev_h1_use[t5] = stt_last
                    ev_w1s_use[wb] = B.now("t")
                    ev_w2s_use[wb] = B.now("t")
                if dbg == "light" and b == NBLK - 1 and e == 0:
                    e_de0 = B.dma("y", out=dbg_aps["d_e0A"].rearrange(
                        "(a p) h -> p a h", p=128),
                        in_=oacc_t.rearrange("p (a h) -> p a h", a=NT128),
                        deps=[stt_last])
                    B.fence("v", [e_de0])
                ev_b2t_use = B.now("t")
            ev_expert_done = B.now("v")
            if dbg == "light":
                e_doa = B.dma("y", out=dbg_aps["d_oaccA"][rows, :].rearrange(
                    "(a p) h -> p a h", p=128),
                    in_=oacc_t.rearrange("p (a h) -> p a h", a=NT128),
                    deps=[ev_expert_done])
                ev_oacc_use = e_doa

            fin_evs = []
            for tt in range(NT128):
                ob = tt % 2
                r128 = slice(b * TB + tt * 128, b * TB + (tt + 1) * 128)
                e_o1 = B.dma("g", out=o1b_t[:, ob * E:(ob + 1) * E],
                             in_=out1_sc.ap()[r128, :],
                             deps=[ev_out1_use, ev_o1b_use[ob]])
                fa = B.op("v", "tensor_tensor", o1b_t[:, ob * E:(ob + 1) * E],
                          o1b_t[:, ob * E:(ob + 1) * E],
                          oacc_t[:, tt * E:(tt + 1) * E],
                          op=ALU.add, deps=[e_o1, ev_expert_done])
                eo = B.dma("g", out=out_ap[r128, :],
                           in_=o1b_t[:, ob * E:(ob + 1) * E], deps=[fa])
                ev_o1b_use[ob] = eo
                ev_out_dma.append(eo)
                fin_evs.append(fa)
            if dbg == "light":
                ev_oacc_use = [fin_evs[-1], e_doa]
            else:
                ev_oacc_use = fin_evs[-1]
            ev_x2lnT_use = B.now("t")

        if dbg == "light":
            B.dma("g", out=dbg_aps["d_w1sc"], in_=w1T_sc.ap()[0][0],
                  deps=B.everything())
            B.dma("g", out=dbg_aps["d_w1bf"],
                  in_=w1_bf.ap()[0][0:1024, 0:128], deps=B.everything())
        B.fence("y", B.everything())
        B.emit(block)
    for ctx in reversed(ps_ctxs):
        ctx.__exit__(None, None, None)
    for ctx in reversed(sb_ctxs):
        ctx.__exit__(None, None, None)
    B.close()
    return nc



def _consts(L):
    posm1 = np.maximum(np.arange(L, dtype=np.float64) - 1.0, 0.0)
    freqs = 10000.0 ** (-np.arange(0, 64, 2, dtype=np.float64) / 64.0)
    ang = freqs[:, None] * posm1[None, :]
    cos = np.cos(ang)
    sin = np.sin(ang)
    cossin = np.concatenate([cos, sin], axis=0).astype(BF)
    cossin_sw = np.concatenate([sin, cos], axis=0).astype(BF)
    return {
        "id64": np.eye(128, dtype=np.float32),
        "h64c": H64_NP,
        "hpc": HP_NP,
        "cossin": np.ascontiguousarray(cossin),
        "cossin_sw": np.ascontiguousarray(cossin_sw),
    }


def kernel(L0_attn_x, L0_attn_w_qkv, L0_attn_w_out, L0_attn_sorf_signs,
           L0_attn_sorf_scales, L0_ffn_router_w, L0_ffn_w1, L0_ffn_b1,
           L0_ffn_w2, L0_ffn_b2):
    L = L0_attn_x.shape[1]
    nc = build_program(L=L)
    consts = _consts(L)
    shared = {
        "w_qkv": np.ascontiguousarray(L0_attn_w_qkv, dtype=np.float32),
        "w_out": np.ascontiguousarray(L0_attn_w_out, dtype=np.float32),
        "sorf_signs": np.ascontiguousarray(L0_attn_sorf_signs, dtype=np.float32),
        "sorf_scales": np.ascontiguousarray(L0_attn_sorf_scales, dtype=np.float32),
        "router_w": np.ascontiguousarray(L0_ffn_router_w, dtype=np.float32),
        "w1": np.ascontiguousarray(L0_ffn_w1, dtype=np.float32),
        "b1": np.ascontiguousarray(L0_ffn_b1, dtype=np.float32),
        "w2": np.ascontiguousarray(L0_ffn_w2, dtype=np.float32),
        "b2": np.ascontiguousarray(L0_ffn_b2, dtype=np.float32),
        **consts,
    }
    in_maps = []
    for i in range(NCORES):
        m = dict(shared)
        m["x"] = np.ascontiguousarray(L0_attn_x[i], dtype=np.float32)
        in_maps.append(m)
    res = run_bass_kernel_spmd(nc, in_maps, core_ids=list(range(NCORES)))
    return np.stack([r["out"] for r in res.results], axis=0)


# version 08
# speedup vs baseline: 1.0661x; 1.0661x over previous
import numpy as np
import ml_dtypes
import concourse.bass as bass
import concourse.mybir as mybir
from concourse.bass_utils import run_bass_kernel_spmd

F32 = mybir.dt.float32
BF16 = mybir.dt.bfloat16
AF = mybir.ActivationFunctionType
ALU = mybir.AluOpType
AX = mybir.AxisListType
BF = ml_dtypes.bfloat16

NCORES = 8
E = 1024
NH = 16
HD = 64
MF = 256
EPS_DENOM = 1e-6 * 256.0
LN_EPS = 1e-5
TWO_PI = float(2.0 * np.pi)
CH = 512


def _hadamard64():
    h = np.array([[1.0]], dtype=np.float64)
    while h.shape[0] < 64:
        h = np.block([[h, h], [h, -h]])
    return h


PERM = np.concatenate([np.arange(0, 64, 2), np.arange(1, 64, 2)])
H64_NP = _hadamard64().astype(np.float32)
HP_NP = np.ascontiguousarray(H64_NP[PERM, :])

ENGINES = ("t", "v", "s", "g", "y")
_BODY = {"t": "tensor", "v": "vector", "s": "scalar", "g": "gpsimd", "y": "sync"}


class Builder:
    NDMA = 8

    def __init__(self, nc):
        self.nc = nc
        self.sems = {}
        self.counts = {}
        self.progs = {en: [] for en in ENGINES}
        self._sem_ctxs = []
        self._dma_rr = {en: 0 for en in ENGINES}
        for en in ENGINES:
            for name in [en] + [f"{en}D{k}" for k in range(self.NDMA)]:
                ctx = nc.semaphore(f"sem_{name}")
                self.sems[name] = ctx.__enter__()
                self._sem_ctxs.append(ctx)
                self.counts[name] = 0
        self.fenced = {en: {} for en in ENGINES}

    def init_consts(self, tile):
        self._ctile = tile
        self._cmap = {}
        self._cn = 0

    def const(self, value):
        if value in self._cmap:
            return self._cmap[value]
        ap = self._ctile[:, self._cn:self._cn + 1]
        self._cn += 1
        ev = self.op("v", "memset", ap, value)
        self.nc.const_aps.register(F32, value, ap)
        self._cmap[value] = ev
        return ev

    def _mkdeps(self, en, deps):
        flat = []
        for d in deps:
            if d is None:
                continue
            if isinstance(d, list):
                flat.extend(x for x in d if x is not None)
            else:
                flat.append(d)
        best = {}
        fen = self.fenced[en]
        for (n, c) in flat:
            if c <= 0:
                continue
            if fen.get(n, 0) >= c:
                continue
            if best.get(n, -1) < c:
                best[n] = c
        for n, c in best.items():
            fen[n] = c
        return tuple(best.items())

    def op(self, en, meth, *args, deps=(), inc=True, **kwargs):
        assert en in ENGINES and en != "y"
        if inc:
            self.counts[en] += 1
        my_inc = 1 if inc else 0
        sems = self.sems
        dd = self._mkdeps(en, deps)

        def emit(eng):
            for (dn, dc) in dd:
                eng.wait_ge(sems[dn], dc)
            inst = getattr(eng, meth)(*args, **kwargs)
            if my_inc:
                inst.then_inc(sems[en], 1)

        self.progs[en].append(emit)
        return (en, self.counts[en])

    def dma(self, en, out, in_, deps=(), **kwargs):
        k = self._dma_rr[en]
        self._dma_rr[en] = (k + 1) % self.NDMA
        dn_name = f"{en}D{k}"
        prev = (dn_name, self.counts[dn_name])
        self.counts[dn_name] += 16
        sems = self.sems
        dd = self._mkdeps(en, tuple(deps) + (prev,))

        def emit(eng):
            for (dn, dc) in dd:
                eng.wait_ge(sems[dn], dc)
            eng.dma_start(out=out, in_=in_, **kwargs).then_inc(sems[dn_name], 16)

        self.progs[en].append(emit)
        return (dn_name, self.counts[dn_name])

    def fence(self, en, deps):
        sems = self.sems
        dd = self._mkdeps(en, deps)
        if not dd:
            return

        def emit(eng):
            for (dn, dc) in dd:
                eng.wait_ge(sems[dn], dc)

        self.progs[en].append(emit)

    def now(self, en):
        return (en, self.counts[en])

    def allD(self, en):
        return [(f"{en}D{k}", self.counts[f"{en}D{k}"]) for k in range(self.NDMA)]

    def everything(self):
        evs = []
        for en in ENGINES:
            if en != "y":
                evs.append(self.now(en))
            evs.extend(self.allD(en))
        return evs

    def emit(self, block):
        for en in ENGINES:
            prog = self.progs[en]

            def runner(eng, prog=prog):
                for fn in prog:
                    fn(eng)

            getattr(block, _BODY[en])(runner)

    def close(self):
        for ctx in reversed(self._sem_ctxs):
            ctx.__exit__(None, None, None)


def dram_ap(t, offset, dims):
    return bass.AP(t, offset, [[s, c] for (s, c) in dims])


class Arena:
    def __init__(self, tensor, cols):
        self.t = tensor
        self.cap = cols
        self.off = 0
        self.marks = {}

    def alloc(self, cols):
        assert self.off + cols <= self.cap, f"arena overflow {self.off}+{cols}>{self.cap}"
        off = self.off
        self.off += cols
        return self.t[:, off:off + cols]

    def mark(self, name):
        self.marks[name] = self.off

    def reset(self, name):
        self.off = self.marks[name]


def build_program(L=8192, dbg=False):
    NCH = L // CH
    TB = min(1024, L)
    NBLK = L // TB
    NT128 = TB // 128

    nc = bass.Bass("TRN2", target_bir_lowering=False)

    x_ap = nc.dram_tensor("x", [L, E], F32, kind="ExternalInput").ap()
    wqkv_t = nc.dram_tensor("w_qkv", [3 * E, E], F32, kind="ExternalInput")
    wout_t = nc.dram_tensor("w_out", [E, E], F32, kind="ExternalInput")
    signs_t = nc.dram_tensor("sorf_signs", [NH, 4, 3, 64], F32, kind="ExternalInput")
    scales_t = nc.dram_tensor("sorf_scales", [NH, 4, 64], F32, kind="ExternalInput")
    rw_t = nc.dram_tensor("router_w", [8, E], F32, kind="ExternalInput")
    w1_t = nc.dram_tensor("w1", [8, 4096, E], F32, kind="ExternalInput")
    b1_t = nc.dram_tensor("b1", [8, 4096], F32, kind="ExternalInput")
    w2_t = nc.dram_tensor("w2", [8, E, 4096], F32, kind="ExternalInput")
    b2_t = nc.dram_tensor("b2", [8, E], F32, kind="ExternalInput")
    id64_ap = nc.dram_tensor("id64", [128, 128], F32, kind="ExternalInput").ap()
    h64_ap = nc.dram_tensor("h64c", [64, 64], F32, kind="ExternalInput").ap()
    hp_ap = nc.dram_tensor("hpc", [64, 64], F32, kind="ExternalInput").ap()
    cossin_ap = nc.dram_tensor("cossin", [64, L], BF16, kind="ExternalInput").ap()
    cossin_sw_ap = nc.dram_tensor("cossin_sw", [64, L], BF16, kind="ExternalInput").ap()

    out_ap = nc.dram_tensor("out", [L, E], F32, kind="ExternalOutput").ap()

    wqkv_bf = nc.dram_tensor("wqkv_bf", [3 * E, E], BF16)
    wout_bf = nc.dram_tensor("wout_bf", [E, E], BF16)
    b2_bf = nc.dram_tensor("b2_bf", [8, E], BF16)
    w1_bf = nc.dram_tensor("w1_bf", [8, 4096, E], BF16)
    w2_bf = nc.dram_tensor("w2_bf", [8, E, 4096], BF16)
    w1T_sc = nc.dram_tensor("w1T_sc", [8, 8, 4, 128, 1024], BF16)
    w2T_sc = nc.dram_tensor("w2T_sc", [8, 32, 128, 1024], BF16)
    xln_sc = nc.dram_tensor("xln_sc", [L, E], BF16)
    cat_sc = nc.dram_tensor("cat_sc", [L, E], BF16)
    out1_sc = nc.dram_tensor("out1_sc", [L, E], F32)
    x2ln_sc = nc.dram_tensor("x2ln_sc", [L, E], BF16)

    dbg_aps = {}
    if dbg == "light":
        dbg_aps["d_gatesA"] = nc.dram_tensor("d_gatesA", [NBLK, 128, 8 * NT128], F32,
                                             kind="ExternalOutput").ap()
        dbg_aps["d_x2A"] = nc.dram_tensor("d_x2A", [L, E], BF16,
                                          kind="ExternalOutput").ap()
        dbg_aps["d_oaccA"] = nc.dram_tensor("d_oaccA", [L, E], F32,
                                            kind="ExternalOutput").ap()
        dbg_aps["d_e0A"] = nc.dram_tensor("d_e0A", [1024, E], F32,
                                          kind="ExternalOutput").ap()
        dbg_aps["d_w1s"] = nc.dram_tensor("d_w1s", [128, 8192], BF16,
                                          kind="ExternalOutput").ap()
        dbg_aps["d_w2s"] = nc.dram_tensor("d_w2s", [128, 8192], BF16,
                                          kind="ExternalOutput").ap()
        dbg_aps["d_h1"] = nc.dram_tensor("d_h1", [128, 4096], BF16,
                                         kind="ExternalOutput").ap()
        dbg_aps["d_w1sc"] = nc.dram_tensor("d_w1sc", [128, 4096], BF16,
                                           kind="ExternalOutput").ap()
        dbg_aps["d_w1bf"] = nc.dram_tensor("d_w1bf", [1024, 128], BF16,
                                           kind="ExternalOutput").ap()
    if dbg is True:
        for name, shape, dt in [
            ("d_xln", [L, E], BF16), ("d_kq", [64, NH * CH], BF16),
            ("d_kphi", [128, NH * MF], BF16), ("d_S", [128, 32 * 65], F32),
            ("d_cat", [L, E], BF16), ("d_out1", [L, E], F32),
            ("d_gates", [128, 8 * NT128], F32), ("d_W", [64, NH * MF], BF16),
            ("d_x2ln", [L, E], BF16),
        ]:
            dbg_aps[name] = nc.dram_tensor(name, shape, dt, kind="ExternalOutput").ap()

    B = Builder(nc)
    sb_ctxs = []

    def sbuf(name, shape, dt):
        ctx = nc.sbuf_tensor(name, shape, dt)
        t = ctx.__enter__()
        sb_ctxs.append(ctx)
        return t

    cpool = sbuf("cpool", [128, 8], F32)
    id128 = sbuf("id128_sb", [128, 128], F32)
    id64 = id128[0:64, 0:64]
    h64 = sbuf("h64_sb", [64, 64], F32)
    hp = sbuf("hp_sb", [64, 64], F32)
    scN = sbuf("scN", [64, 192], F32)
    scP = sbuf("scP", [64, 192], F32)
    gcols = sbuf("gcols", [64, 64], F32)
    sctmp = sbuf("sctmp", [96, 64], F32)
    sperm = sbuf("sperm", [96, 64], F32)
    wtmp = sbuf("wtmp", [64, 192], F32)
    W_sb = sbuf("W_sb", [64, NH * MF], BF16)
    sincos = sbuf("sincos", [64, CH], BF16)
    sincos_sw = sbuf("sincos_sw", [64, CH], BF16)
    sc_f32 = sbuf("sc_f32", [64, 2 * CH], F32)
    rwT = sbuf("rwT", [128, 64], BF16)
    rw_nat = sbuf("rw_nat", [8, E], F32)
    b1T = sbuf("b1T", [128, 8 * 32], F32)
    b1tmp = sbuf("b1tmp", [32, 128], F32)
    b2t = sbuf("b2t", [1, E], BF16)
    ones1 = sbuf("ones1", [1, 128], BF16)
    halfc = sbuf("halfc", [64, 1], BF16)
    hi_t = sbuf("hi_t", [1, CH], BF16)
    lo_t = sbuf("lo_t", [1, CH], BF16)
    nsqf = sbuf("nsqf", [1, CH], F32)
    stat = sbuf("stat", [128, 16], F32)
    nsqneg = sbuf("nsqneg", [128, 64], F32)
    dcol = sbuf("dcol", [128, 2], F32)
    S_sb = sbuf("S_sb", [128, 32 * 65], F32)
    S_bf = sbuf("S_bf", [128, 32 * 65], BF16)
    gates = sbuf("gates", [128, 8 * NT128], F32)
    gtmp = sbuf("gtmp", [128, 8], F32)
    gcol = sbuf("gcol", [128, 4], F32)

    abf_t = sbuf("arena_bf", [128, 61440], BF16)
    af_t = sbuf("arena_f32", [128, 10240], F32)
    ABF = Arena(abf_t, 61440)
    AFA = Arena(af_t, 10240)

    ps_ctxs = []
    P = []
    for i in range(8):
        ctx = nc.psum_tensor(f"P{i}", [128, 512], F32)
        P.append(ctx.__enter__())
        ps_ctxs.append(ctx)

    B.init_consts(cpool)

    with nc.Block() as block:
        ev_eps = B.const(LN_EPS)
        e_id = B.dma("g", out=id128[:, :], in_=id64_ap)
        e_h64 = B.dma("g", out=h64[:, :], in_=h64_ap)
        e_hp = B.dma("g", out=hp[:, :], in_=hp_ap)
        e_rw = B.dma("g", out=rw_nat[:, :], in_=rw_t.ap())
        ev_ones = B.op("v", "memset", ones1[:, :], 1.0)
        ev_half = B.op("v", "memset", halfc[:, :], 0.5)

        e_wqk0 = B.dma("g",
                       out=dram_ap(wqkv_bf, 0,
                                   [(64 * E, 32), (E, 32), (1, E)]),
                       in_=dram_ap(wqkv_t, 0,
                                   [(64 * E, 32), (2 * E, 32), (1, E)]))
        e_wqk1 = B.dma("g",
                       out=dram_ap(wqkv_bf, 32 * E,
                                   [(64 * E, 32), (E, 32), (1, E)]),
                       in_=dram_ap(wqkv_t, E,
                                   [(64 * E, 32), (2 * E, 32), (1, E)]))
        e_wqk = [e_wqk0, e_wqk1]
        e_wv = B.dma("g", out=wqkv_bf.ap()[2048:3072, :],
                     in_=wqkv_t.ap()[2048:3072, :])
        e_wout = B.dma("g", out=wout_bf.ap()[:, :], in_=wout_t.ap()[:, :])
        e_b2c = B.dma("g", out=b2_bf.ap()[:, :], in_=b2_t.ap()[:, :])
        e_w1c, e_w2c = [], []
        for e in range(8):
            e_w1c.append(B.dma("g", out=w1_bf.ap()[e], in_=w1_t.ap()[e]))
            e_w2c.append(B.dma("g", out=w2_bf.ap()[e], in_=w2_t.ap()[e]))

        def sign_half(row0, ev_load, pslot):
            eT = B.op("t", "transpose", P[pslot][0:64, 0:96], sctmp[0:96, 0:64],
                      id128[0:96, 0:96], deps=[ev_load, e_id])
            eN = B.op("v", "tensor_copy", scN[:, row0:row0 + 96],
                      P[pslot][0:64, 0:96], deps=[eT])
            eP0 = B.op("v", "tensor_copy",
                       sperm[0:96, 0:64].rearrange("r (par j) -> r par j", par=2),
                       sctmp[0:96, 0:64].rearrange("r (j par) -> r par j", par=2),
                       deps=[ev_load])
            eT2 = B.op("t", "transpose", P[pslot][0:64, 96:192], sperm[0:96, 0:64],
                       id128[0:96, 0:96], deps=[eP0, e_id])
            eP = B.op("v", "tensor_copy", scP[:, row0:row0 + 96],
                      P[pslot][0:64, 96:192], deps=[eT2])
            return eN, eP

        e_sgA = B.dma("g", out=sctmp[0:96, :],
                      in_=signs_t.ap().rearrange("h r p d -> (h r p) d")[0:96, :])
        eN_a, eP_a = sign_half(0, e_sgA, 7)
        e_sgB = B.dma("g", out=sctmp[0:96, :],
                      in_=signs_t.ap().rearrange("h r p d -> (h r p) d")[96:192, :],
                      deps=[eN_a, eP_a])
        eN_b, eP_b = sign_half(96, e_sgB, 7)
        e_sc = B.dma("g", out=sctmp[0:64, :],
                     in_=scales_t.ap().rearrange("h r d -> (h r) d"),
                     deps=[eN_b, eP_b])
        eTsc = B.op("t", "transpose", P[7][0:64, 192:256], sctmp[0:64, 0:64],
                    id64[:, :], deps=[e_sc, e_id])
        e_gc = B.op("v", "tensor_copy", gcols[:, :], P[7][0:64, 192:256], deps=[eTsc])

        Z1 = wtmp[:, 0:64]
        Z2 = wtmp[:, 64:128]
        Z3 = wtmp[:, 128:192]
        prevW = None
        for h in range(NH):
            for r in range(4):
                idx = h * 4 + r
                s1c = scP[:, idx * 3 + 0:idx * 3 + 1]
                s2c = scN[:, idx * 3 + 1:idx * 3 + 2]
                s3c = scN[:, idx * 3 + 2:idx * 3 + 3]
                gc = gcols[:, idx:idx + 1]
                e1 = B.op("v", "tensor_scalar", Z1, h64[:, :], gc, None,
                          op0=ALU.mult, deps=[e_gc, e_h64, prevW])
                e2 = B.op("t", "transpose", P[6][0:64, 0:64], Z1, id64[:, :],
                          deps=[e1, e_id])
                e3 = B.op("v", "tensor_scalar", Z3, P[6][0:64, 0:64], s3c, None,
                          op0=ALU.mult, deps=[e2, eN_b])
                e4 = B.op("v", "tensor_scalar", Z2, h64[:, :], s2c, None,
                          op0=ALU.mult, deps=[e3])
                e5 = B.op("t", "transpose", P[6][0:64, 64:128], Z2, id64[:, :],
                          deps=[e4])
                e5c = B.op("v", "tensor_copy", Z2, P[6][0:64, 64:128], deps=[e5])
                e6 = B.op("t", "matmul", P[6][0:64, 128:192], Z2, Z3,
                          start=True, stop=True, deps=[e5c, e3])
                e6c = B.op("v", "tensor_copy", Z3, P[6][0:64, 128:192], deps=[e6])
                e7 = B.op("v", "tensor_scalar", Z1, hp[:, :], s1c, None,
                          op0=ALU.mult, deps=[eP_b, e_hp, e6])
                e8 = B.op("t", "transpose", P[6][0:64, 192:256], Z1, id64[:, :],
                          deps=[e7])
                e8c = B.op("v", "tensor_copy", Z1, P[6][0:64, 192:256], deps=[e8])
                e9 = B.op("t", "matmul", P[6][0:64, 256:320], Z1, Z3,
                          start=True, stop=True, deps=[e8c, e6c])
                prevW = B.op("v", "tensor_scalar",
                             W_sb[:, h * MF + r * 64: h * MF + (r + 1) * 64],
                             P[6][0:64, 256:320], 1.0 / 4096.0, None,
                             op0=ALU.mult, deps=[e9])
        ev_W = prevW
        if dbg is True:
            B.dma("y", out=dbg_aps["d_W"], in_=W_sb[:, :], deps=[ev_W])

        prev_rw = None
        for hc in range(8):
            eT = B.op("t", "transpose", P[7][:, 8:16],
                      rw_nat[0:8, hc * 128:(hc + 1) * 128], id64[0:8, 0:8],
                      deps=[e_rw, e_id, prev_rw, e_gc])
            prev_rw = B.op("v", "tensor_copy", rwT[:, hc * 8:(hc + 1) * 8],
                           P[7][:, 8:16], deps=[eT])
        ev_rwT = prev_rw
        prev_b1 = None
        for e in range(8):
            eL = B.dma("g", out=b1tmp[0:32, :],
                       in_=b1_t.ap()[e].rearrange("(fc p) -> fc p", p=128),
                       deps=[prev_b1])
            eT = B.op("t", "transpose", P[7][:, 16:48], b1tmp[0:32, 0:128],
                      id64[0:32, 0:32], deps=[eL, e_id, ev_rwT, prev_b1])
            prev_b1 = B.op("v", "tensor_copy", b1T[:, e * 32:(e + 1) * 32],
                           P[7][:, 16:48], deps=[eT])
        ev_b1T = prev_b1

        stg = [ABF.alloc(2048), ABF.alloc(2048)]
        stg2 = [ABF.alloc(2048), ABF.alloc(2048)]
        stg_free = [None, None]
        stg2_free = [None, None]
        si = 0
        ev_w1T = {}
        ev_w2T = {}
        def prep_tile(job):
            nonlocal si
            in_ap, out_ap, dep_load, e, kind = job
            st = stg[si % 2]
            st2 = stg2[si % 2]
            eT = B.dma("y", out=st[:, 0:1024], in_=in_ap, transpose=True,
                       deps=[dep_load, stg_free[si % 2]])
            eC = B.op("s", "copy", st2[:, 0:1024], st[:, 0:1024],
                      deps=[eT, stg2_free[si % 2]])
            stg_free[si % 2] = eC
            eW = B.dma("g", out=out_ap, in_=st2[:, 0:1024], deps=[eC])
            stg2_free[si % 2] = eW
            si += 1
            if kind == 1:
                ev_w1T[e] = eW
            else:
                ev_w2T[e] = eW

        prep_jobs = []
        for e in range(8):
            for hc in range(8):
                for q in range(4):
                    prep_jobs.append((
                        w1_bf.ap()[e][q * 1024:(q + 1) * 1024,
                                      hc * 128:(hc + 1) * 128],
                        w1T_sc.ap()[e][hc][q], e_w1c[e], e, 1))
            for fc in range(32):
                prep_jobs.append((
                    w2_bf.ap()[e][:, fc * 128:(fc + 1) * 128],
                    w2T_sc.ap()[e][fc], e_w2c[e], e, 2))
        prep_jobs.reverse()

        def run_prep_jobs(n):
            for _ in range(n):
                if prep_jobs:
                    prep_tile(prep_jobs.pop())
        ABF.mark("attn")
        AFA.mark("attn")

        xln_t = ABF.alloc(4 * E)
        xlnT_t = [ABF.alloc(8 * CH), ABF.alloc(8 * CH)]
        wbig_t = ABF.alloc(8 * 2048)
        kq_t = ABF.alloc(NH * CH)
        ksq_t = ABF.alloc(2 * CH)
        _u1 = ABF.alloc(max(NH * MF, 4 * E))
        kphi_t = _u1[:, 0:NH * MF]
        cat_t = _u1[:, 0:4 * E]
        _u2 = ABF.alloc(max(4 * NH * 65, 4 * E))
        v_t = _u2[:, 0:4 * NH * 65]
        x2ln_t = _u2[:, 0:4 * E]
        qphi_t = ABF.alloc(2 * 1024)
        catT_t = ABF.alloc(8 * CH)
        x_t = AFA.alloc(4 * E)
        out1_t = AFA.alloc(4 * E)

        def gen_sincos(c, deps_extra):
            e1 = B.dma("g", out=sincos[:, :],
                       in_=cossin_ap[:, c * CH:(c + 1) * CH], deps=deps_extra)
            e2 = B.dma("g", out=sincos_sw[:, :],
                       in_=cossin_sw_ap[:, c * CH:(c + 1) * CH], deps=deps_extra)
            return [e1, e2]

        def layer_norm(src_f32, dst_bf16, deps):
            s0 = B.op("v", "bn_stats", stat[:, 0:6], src_f32[:, 0:512], deps=deps)
            s1 = B.op("v", "bn_stats", stat[:, 6:12], src_f32[:, 512:1024], deps=deps)
            s2 = B.op("v", "bn_aggr", stat[:, 12:14], stat[:, 0:12], deps=[s0, s1])
            s3 = B.op("s", "activation", stat[:, 14:15], stat[:, 13:14], AF.Sqrt,
                      bias=LN_EPS, deps=[s2, ev_eps])
            s4 = B.op("v", "reciprocal", stat[:, 14:15], stat[:, 14:15], deps=[s3])
            return B.op("v", "tensor_scalar", dst_bf16, src_f32,
                        stat[:, 12:13], stat[:, 14:15],
                        op0=ALU.subtract, op1=ALU.mult, deps=[s4])

        ev_x_use = None
        ev_xln_use = None
        ev_xlnT_use = [None, None]
        ev_kq_use = None
        ev_ksq_use = [None, None]
        ev_kphi_use = None
        ev_v_use = None
        ev_ps = [None] * 8
        ev_rope_prev = {"v": None, "g": None}
        ev_ps[6] = ev_W
        ev_ps[7] = ev_b1T
        ev_nsq_evac = ev_W
        ev_sincos_use = None

        ev_qphi_use = [None, None]
        ev_S0 = B.op("v", "memset", S_sb[:, :], 0.0)
        vz = B.op("v", "memset", v_t[:, :], 0.0)
        one_evs = []
        for tt in range(4):
            for h in range(NH):
                col = tt * NH * 65 + h * 65 + 64
                one_evs.append(B.op("v", "memset", v_t[:, col:col + 1], 1.0,
                                    deps=[vz]))
        ev_v_use = one_evs[-1]

        ev_wbigK = []
        for hc in range(8):
            ev = B.dma("y", out=wbig_t[:, hc * 2048:(hc + 1) * 2048],
                       in_=wqkv_bf.ap()[1024:3072, hc * 128:(hc + 1) * 128],
                       transpose=True,
                       deps=[e_wqk, e_wv])
            ev_wbigK.append(ev)

        for c in range(NCH):
            rows = slice(c * CH, (c + 1) * CH)
            e_x = B.dma("y", out=x_t.rearrange("p (a h) -> p a h", a=4),
                        in_=x_ap[rows, :].rearrange("(a p) h -> p a h", p=128),
                        deps=[ev_x_use])
            ev_sc = gen_sincos(c, [ev_sincos_use, ev_ps[7]])
            ln_evs = []
            for a in range(4):
                ln_evs.append(layer_norm(x_t[:, a * E:(a + 1) * E],
                                         xln_t[:, a * E:(a + 1) * E],
                                         [e_x, ev_xln_use]))
            ev_x_use = ln_evs[-1]
            e_wx = B.dma("g",
                         out=xln_sc.ap()[rows, :].rearrange("(a p) h -> p a h", p=128),
                         in_=xln_t.rearrange("p (a h) -> p a h", a=4),
                         deps=ln_evs)
            ev_xln_use = e_wx
            if dbg is True:
                _ed = B.dma("y", out=dbg_aps["d_xln"][rows, :].rearrange("(a p) h -> p a h", p=128),
                      in_=xln_t.rearrange("p (a h) -> p a h", a=4), deps=ln_evs)
                B.fence("v", [_ed])

            buf = c % 2
            exT = []
            for hc in range(8):
                ev = B.dma("y", out=xlnT_t[buf][:, hc * CH:(hc + 1) * CH],
                           in_=xln_sc.ap()[rows, hc * 128:(hc + 1) * 128],
                           transpose=True, deps=[e_wx, ev_xlnT_use[buf]])
                exT.append(ev)
            xlnT = xlnT_t[buf]
            B.fence("t", exT + ev_wbigK + [ev_W])

            kq_evs = {}
            for ic in range(8):
                pb = ic % 2
                B.fence("t", [ev_ps[pb]])
                mm = None
                for hc in range(8):
                    mm = B.op("t", "matmul", P[pb][:, 0:CH],
                              wbig_t[:, hc * 2048 + ic * 128: hc * 2048 + (ic + 1) * 128],
                              xlnT[:, hc * CH:(hc + 1) * CH],
                              start=(hc == 0), stop=(hc == 7), inc=(hc == 7))
                for half in range(2):
                    h = 2 * ic + half
                    ev = B.op("s", "copy", kq_t[0:64, h * CH:(h + 1) * CH],
                              P[pb][64 * half:64 * (half + 1), 0:CH],
                              deps=[mm, ev_kq_use])
                    kq_evs[h] = ev
                ev_ps[pb] = kq_evs[2 * ic + 1]

            v_evs = [None] * 4
            for tt in range(4):
                for half in range(2):
                    pb = tt % 2
                    B.fence("t", [ev_ps[pb]])
                    mm = None
                    for hc in range(8):
                        mm = B.op("t", "matmul", P[pb][:, 0:CH],
                                  xlnT[:, hc * CH + tt * 128: hc * CH + (tt + 1) * 128],
                                  wbig_t[:, hc * 2048 + 1024 + half * CH:
                                         hc * 2048 + 1024 + (half + 1) * CH],
                                  start=(hc == 0), stop=(hc == 7), inc=(hc == 7))
                    base = tt * NH * 65 + half * 8 * 65
                    ev = B.op("s", "copy",
                              v_t[:, base:base + 8 * 65].rearrange(
                                  "p (h d) -> p h d", d=65)[:, :, 0:64],
                              P[pb][:, 0:CH].rearrange("p (h d) -> p h d", d=64),
                              deps=[mm, ev_v_use])
                    ev_ps[pb] = ev
                    v_evs[tt] = ev

            rope_map = {}
            B.fence("v", [ev_sc])
            B.fence("g", [ev_sc, ev_qphi_use[0], ev_qphi_use[1]])
            prev_rope = ev_rope_prev
            for h in range(NH):
                hs = slice(h * CH, (h + 1) * CH)
                evh = kq_evs[h]
                en = "g" if h % 3 == 2 else "v"
                if en == "v":
                    q1 = sc_f32[0:32, 0:CH]
                    q2 = sc_f32[0:32, CH:2 * CH]
                    q3 = sc_f32[32:64, CH:2 * CH]
                    q4 = sc_f32[32:64, 0:CH]
                else:
                    q1 = qphi_t[0:32, 0:CH]
                    q2 = qphi_t[0:32, CH:2 * CH]
                    q3 = qphi_t[32:64, CH:2 * CH]
                    q4 = qphi_t[32:64, 0:CH]
                t1 = B.op(en, "tensor_tensor", q1, kq_t[0:32, hs],
                          sincos[0:32, :], op=ALU.mult,
                          deps=[evh, prev_rope[en]])
                t2 = B.op(en, "tensor_tensor", q2, kq_t[32:64, hs],
                          sincos[32:64, :], op=ALU.mult, deps=[evh])
                t3 = B.op(en, "tensor_tensor", q3, kq_t[0:32, hs],
                          sincos_sw[0:32, :], op=ALU.mult, deps=[evh])
                t4 = B.op(en, "tensor_tensor", q4, kq_t[32:64, hs],
                          sincos_sw[32:64, :], op=ALU.mult, deps=[evh])
                o1 = B.op(en, "tensor_tensor", kq_t[0:32, hs], q1, q2,
                          op=ALU.subtract, deps=[t4])
                o2 = B.op(en, "tensor_tensor", kq_t[32:64, hs], q3, q4,
                          op=ALU.add, deps=[t4])
                rope_map[h] = (o1, o2)
                prev_rope[en] = o2
            ev_sincos_use = [rope_map[NH - 2][1], rope_map[NH - 1][1]]
            if dbg is True and c == 0:
                _ed = B.dma("y", out=dbg_aps["d_kq"], in_=kq_t[0:64, :],
                      deps=[e for pair in rope_map.values() for e in pair])
                B.fence("s", [_ed])

            nsq_mm_last = None
            sq_map = {}
            for h in range(NH):
                o1, o2 = rope_map[h]
                sqb = h % 2
                esq = B.op("s", "activation", ksq_t[0:64, sqb * CH:(sqb + 1) * CH],
                           kq_t[0:64, h * CH:(h + 1) * CH], AF.Square,
                           deps=[o1, o2, ev_ksq_use[sqb]])
                B.fence("t", [esq, ev_half, ev_nsq_evac])
                for tt in range(4):
                    col = h * 4 + tt
                    nsq_mm_last = B.op("t", "matmul", P[6][:, col:col + 1],
                                       ksq_t[0:64, sqb * CH + tt * 128:
                                             sqb * CH + (tt + 1) * 128],
                                       halfc[:, :], start=True, stop=True,
                                       inc=(tt == 3))
                sq_map[h] = nsq_mm_last
                ev_ksq_use[sqb] = nsq_mm_last
            e_nsqe = B.op("v", "tensor_scalar", nsqneg[:, :], P[6][:, 0:64],
                          -1.0, None, op0=ALU.mult, deps=[nsq_mm_last])
            ev_nsq_evac = e_nsqe

            for tt in range(4):
                phi_evs = {}
                for h in range(NH):
                    pb = 2 + (h % 2)
                    o1, o2 = rope_map[h]
                    B.fence("t", [ev_ps[pb]])
                    mmp = B.op("t", "matmul", P[pb][:, 0:MF],
                               kq_t[0:64, h * CH + tt * 128: h * CH + (tt + 1) * 128],
                               W_sb[:, h * MF:(h + 1) * MF],
                               start=True, stop=True, deps=[o1, o2])
                    eex = B.op("s", "activation", kphi_t[:, h * MF:(h + 1) * MF],
                               P[pb][:, 0:MF], AF.Exp,
                               bias=nsqneg[:, h * 4 + tt: h * 4 + tt + 1],
                               deps=[mmp, e_nsqe, ev_kphi_use])
                    ev_ps[pb] = eex
                    phi_evs[h] = eex
                if dbg is True and c == 0 and tt == 0:
                    _ed = B.dma("y", out=dbg_aps["d_kphi"], in_=kphi_t[:, :],
                          deps=list(phi_evs.values()))
                    B.fence("s", [_ed])
                groups = [(h, mc) for h in range(NH) for mc in range(2)]
                s_last = None
                for gi in range(0, 32, 14):
                    wave = groups[gi:gi + 14]
                    nbank = [min(7, len(wave)), max(0, len(wave) - 7)]
                    bank_ev = [None, None]
                    for bi in range(2):
                        if nbank[bi] == 0:
                            continue
                        bank = 4 + bi
                        B.fence("t", [v_evs[tt], ev_ps[bank]] +
                                [phi_evs[h] for (h, mc) in wave])
                        for wj in range(nbank[bi]):
                            h, mc = wave[bi * 7 + wj]
                            bank_ev[bi] = B.op(
                                "t", "matmul", P[bank][:, wj * 65:(wj + 1) * 65],
                                kphi_t[:, h * MF + mc * 128: h * MF + (mc + 1) * 128],
                                v_t[:, tt * NH * 65 + h * 65:
                                    tt * NH * 65 + (h + 1) * 65],
                                start=True, stop=True, inc=(wj == nbank[bi] - 1))
                    for bi in range(2):
                        for wj in range(nbank[bi]):
                            h, mc = wave[bi * 7 + wj]
                            bank = 4 + bi
                            scol = (h * 2 + mc) * 65
                            s_last = B.op("v", "tensor_tensor",
                                          S_sb[:, scol:scol + 65],
                                          S_sb[:, scol:scol + 65],
                                          P[bank][:, wj * 65:(wj + 1) * 65],
                                          op=ALU.add, deps=[bank_ev[bi], ev_S0])
                        ev_ps[4 + bi] = s_last
                ev_kphi_use = s_last
                ev_v_use = s_last
            ev_xlnT_use[buf] = B.now("t")
            run_prep_jobs(11 if c > 0 else 0)

        ev_S_done = ev_kphi_use
        if dbg is True:
            _ed = B.dma("y", out=dbg_aps["d_S"], in_=S_sb[:, :], deps=[ev_S_done])
            B.fence("v", [_ed])

        e_Sbf = B.op("v", "tensor_copy", S_bf[:, :], S_sb[:, :], deps=[ev_S_done])

        ev_wbigQ = []
        for hc in range(8):
            ev = B.dma("y", out=wbig_t[:, hc * 2048: hc * 2048 + 1024],
                       in_=wqkv_bf.ap()[0:1024, hc * 128:(hc + 1) * 128],
                       transpose=True, deps=[e_wqk, ev_xlnT_use[0], ev_xlnT_use[1],
                                             B.now("t")])
            ev_wbigQ.append(ev)
            ev = B.dma("y", out=wbig_t[:, hc * 2048 + 1024:(hc + 1) * 2048],
                       in_=wout_bf.ap()[0:1024, hc * 128:(hc + 1) * 128],
                       transpose=True, deps=[e_wout])
            ev_wbigQ.append(ev)

        ev_cat_use = None
        ev_catT_use = None
        ev_out1_use = None
        ev_x2ln_use = None
        ev_hilo_use = None

        for c in range(NCH):
            rows = slice(c * CH, (c + 1) * CH)
            e_x = B.dma("y", out=x_t.rearrange("p (a h) -> p a h", a=4),
                        in_=x_ap[rows, :].rearrange("(a p) h -> p a h", p=128),
                        deps=[ev_x_use])
            ev_sc = gen_sincos(c, [ev_sincos_use, ev_ps[7]])
            buf = c % 2
            exT = []
            for hc in range(8):
                ev = B.dma("y", out=xlnT_t[buf][:, hc * CH:(hc + 1) * CH],
                           in_=xln_sc.ap()[rows, hc * 128:(hc + 1) * 128],
                           transpose=True, deps=[ev_xln_use, ev_xlnT_use[buf]])
                exT.append(ev)
            xlnT = xlnT_t[buf]
            B.fence("t", exT + ev_wbigQ + [e_Sbf])

            kq_evs = {}
            for ic in range(8):
                pb = ic % 2
                B.fence("t", [ev_ps[pb]])
                mm = None
                for hc in range(8):
                    mm = B.op("t", "matmul", P[pb][:, 0:CH],
                              wbig_t[:, hc * 2048 + ic * 128: hc * 2048 + (ic + 1) * 128],
                              xlnT[:, hc * CH:(hc + 1) * CH],
                              start=(hc == 0), stop=(hc == 7), inc=(hc == 7))
                for half in range(2):
                    h = 2 * ic + half
                    kq_evs[h] = B.op("s", "copy", kq_t[0:64, h * CH:(h + 1) * CH],
                                     P[pb][64 * half:64 * (half + 1), 0:CH],
                                     deps=[mm, ev_kq_use])
                ev_ps[pb] = kq_evs[2 * ic + 1]
            rope_map = {}
            B.fence("v", [ev_sc])
            B.fence("g", [ev_sc, ev_qphi_use[0], ev_qphi_use[1]])
            prev_rope = ev_rope_prev
            for h in range(NH):
                hs = slice(h * CH, (h + 1) * CH)
                evh = kq_evs[h]
                en = "g" if h % 3 == 2 else "v"
                if en == "v":
                    q1 = sc_f32[0:32, 0:CH]
                    q2 = sc_f32[0:32, CH:2 * CH]
                    q3 = sc_f32[32:64, CH:2 * CH]
                    q4 = sc_f32[32:64, 0:CH]
                else:
                    q1 = qphi_t[0:32, 0:CH]
                    q2 = qphi_t[0:32, CH:2 * CH]
                    q3 = qphi_t[32:64, CH:2 * CH]
                    q4 = qphi_t[32:64, 0:CH]
                t1 = B.op(en, "tensor_tensor", q1, kq_t[0:32, hs],
                          sincos[0:32, :], op=ALU.mult,
                          deps=[evh, prev_rope[en]])
                t2 = B.op(en, "tensor_tensor", q2, kq_t[32:64, hs],
                          sincos[32:64, :], op=ALU.mult, deps=[evh])
                t3 = B.op(en, "tensor_tensor", q3, kq_t[0:32, hs],
                          sincos_sw[0:32, :], op=ALU.mult, deps=[evh])
                t4 = B.op(en, "tensor_tensor", q4, kq_t[32:64, hs],
                          sincos_sw[32:64, :], op=ALU.mult, deps=[evh])
                o1 = B.op(en, "tensor_tensor", kq_t[0:32, hs], q1, q2,
                          op=ALU.subtract, deps=[t4])
                o2 = B.op(en, "tensor_tensor", kq_t[32:64, hs], q3, q4,
                          op=ALU.add, deps=[t4])
                rope_map[h] = (o1, o2)
                prev_rope[en] = o2
            ev_sincos_use = [rope_map[NH - 2][1], rope_map[NH - 1][1]]

            cat_evs = []
            for h in range(NH):
                o1, o2 = rope_map[h]
                sqb = h % 2
                esq = B.op("s", "activation", ksq_t[0:64, sqb * CH:(sqb + 1) * CH],
                           kq_t[0:64, h * CH:(h + 1) * CH], AF.Square,
                           deps=[o1, o2, ev_ksq_use[sqb]])
                nb = 4 + (h % 2)
                B.fence("t", [esq, ev_half, ev_ps[nb]])
                mmn = B.op("t", "matmul", P[nb][0:1, 0:CH], halfc[:, :],
                           ksq_t[0:64, sqb * CH:(sqb + 1) * CH],
                           start=True, stop=True)
                ev_ksq_use[sqb] = mmn
                en1 = B.op("v", "tensor_scalar", nsqf[:, :], P[nb][0:1, 0:CH],
                           -1.0, None, op0=ALU.mult, deps=[mmn, ev_hilo_use])
                en2 = B.op("v", "tensor_copy", hi_t[:, :], nsqf[:, :], deps=[en1])
                en3 = B.op("v", "tensor_tensor", lo_t[:, :], nsqf[:, :], hi_t[:, :],
                           op=ALU.subtract, deps=[en2])
                ev_ps[nb] = en1
                qb = h % 2
                phi_evs = []
                for mc in range(2):
                    pb = 2 + mc
                    B.fence("t", [ev_ps[pb], en3, ev_ones])
                    m1 = B.op("t", "matmul", P[pb][:, 0:CH],
                              W_sb[:, h * MF + mc * 128: h * MF + (mc + 1) * 128],
                              kq_t[0:64, h * CH:(h + 1) * CH],
                              start=True, stop=False, inc=False)
                    m2 = B.op("t", "matmul", P[pb][:, 0:CH], ones1[:, :],
                              hi_t[:, :], start=False, stop=False, inc=False)
                    m3 = B.op("t", "matmul", P[pb][:, 0:CH], ones1[:, :],
                              lo_t[:, :], start=False, stop=True, inc=True)
                    eex = B.op("s", "activation",
                               qphi_t[:, qb * 1024 + mc * CH: qb * 1024 + (mc + 1) * CH],
                               P[pb][:, 0:CH], AF.Exp,
                               deps=[m3, ev_qphi_use[qb], prev_rope["g"],
                                     prev_rope["v"]])
                    ev_ps[pb] = eex
                    phi_evs.append(eex)
                ev_hilo_use = phi_evs[-1]
                num_last = None
                for tt in range(4):
                    nb2 = 6 + (tt % 2)
                    B.fence("t", phi_evs + [e_Sbf, ev_ps[nb2]])
                    mm = None
                    for mc in range(2):
                        mm = B.op("t", "matmul", P[nb2][:, 0:65],
                                  qphi_t[:, qb * 1024 + mc * CH + tt * 128:
                                         qb * 1024 + mc * CH + (tt + 1) * 128],
                                  S_bf[:, (h * 2 + mc) * 65:(h * 2 + mc + 1) * 65],
                                  start=(mc == 0), stop=(mc == 1), inc=(mc == 1))
                    d1 = B.op("v", "tensor_scalar", dcol[:, 0:1], P[nb2][:, 64:65],
                              EPS_DENOM, None, op0=ALU.max, deps=[mm])
                    d2 = B.op("v", "reciprocal", dcol[:, 1:2], dcol[:, 0:1], deps=[d1])
                    d3 = B.op("v", "tensor_scalar",
                              cat_t[:, tt * E + h * 64: tt * E + (h + 1) * 64],
                              P[nb2][:, 0:64], dcol[:, 1:2], None,
                              op0=ALU.mult, deps=[d2, ev_cat_use])
                    ev_ps[nb2] = d3
                    num_last = d3
                    cat_evs.append(d3)
                ev_qphi_use[qb] = num_last
            ev_kq_use = cat_evs[-1]

            e_wc = B.dma("g",
                         out=cat_sc.ap()[rows, :].rearrange("(a p) h -> p a h", p=128),
                         in_=cat_t.rearrange("p (a h) -> p a h", a=4),
                         deps=cat_evs)
            ev_cat_use = e_wc
            if dbg is True:
                _ed = B.dma("y", out=dbg_aps["d_cat"][rows, :].rearrange("(a p) h -> p a h", p=128),
                      in_=cat_t.rearrange("p (a h) -> p a h", a=4), deps=cat_evs)
                B.fence("v", [_ed])
            ecT = []
            for hc in range(8):
                ev = B.dma("y", out=catT_t[:, hc * CH:(hc + 1) * CH],
                           in_=cat_sc.ap()[rows, hc * 128:(hc + 1) * 128],
                           transpose=True, deps=[e_wc, ev_catT_use])
                ecT.append(ev)
            B.fence("t", ecT)
            op_evs = []
            ln2_evs = []
            for tt in range(4):
                for half in range(2):
                    pb = half
                    B.fence("t", [ev_ps[pb]])
                    mm = None
                    for hc in range(8):
                        mm = B.op("t", "matmul", P[pb][:, 0:CH],
                                  catT_t[:, hc * CH + tt * 128: hc * CH + (tt + 1) * 128],
                                  wbig_t[:, hc * 2048 + 1024 + half * CH:
                                         hc * 2048 + 1024 + (half + 1) * CH],
                                  start=(hc == 0), stop=(hc == 7), inc=(hc == 7))
                    ev = B.op("v", "tensor_tensor",
                              out1_t[:, tt * E + half * CH: tt * E + (half + 1) * CH],
                              P[pb][:, 0:CH],
                              x_t[:, tt * E + half * CH: tt * E + (half + 1) * CH],
                              op=ALU.add, deps=[mm, e_x, ev_out1_use])
                    ev_ps[pb] = ev
                    op_evs.append(ev)
                ln2_evs.append(layer_norm(out1_t[:, tt * E:(tt + 1) * E],
                                          x2ln_t[:, tt * E:(tt + 1) * E],
                                          [op_evs[-1], op_evs[-2], ev_x2ln_use]))
            ev_x_use = op_evs[-1]
            ev_catT_use = B.now("t")
            e_wo1 = B.dma("g",
                          out=out1_sc.ap()[rows, :].rearrange("(a p) h -> p a h", p=128),
                          in_=out1_t.rearrange("p (a h) -> p a h", a=4),
                          deps=op_evs)
            ev_out1_use = e_wo1
            e_wx2 = B.dma("g",
                          out=x2ln_sc.ap()[rows, :].rearrange("(a p) h -> p a h", p=128),
                          in_=x2ln_t.rearrange("p (a h) -> p a h", a=4),
                          deps=ln2_evs)
            ev_x2ln_use = e_wx2
            if dbg == "light":
                e_dx2 = B.dma("y", out=dbg_aps["d_x2A"][rows, :].rearrange(
                    "(a p) h -> p a h", p=128),
                    in_=x2ln_t.rearrange("p (a h) -> p a h", a=4), deps=ln2_evs)
                ev_x2ln_use = [e_wx2, e_dx2]
            run_prep_jobs(11)
            if dbg is True:
                _ed = B.dma("y", out=dbg_aps["d_out1"][rows, :].rearrange("(a p) h -> p a h", p=128),
                      in_=out1_t.rearrange("p (a h) -> p a h", a=4), deps=op_evs)
                B.fence("v", [_ed])
                _ed = B.dma("y", out=dbg_aps["d_x2ln"][rows, :].rearrange("(a p) h -> p a h", p=128),
                      in_=x2ln_t.rearrange("p (a h) -> p a h", a=4), deps=ln2_evs)
                B.fence("v", [_ed])

        run_prep_jobs(len(prep_jobs))
        for en in ENGINES:
            B.fence(en, B.everything())
        ABF.reset("attn")
        AFA.reset("attn")
        x2lnT_t = ABF.alloc(8 * TB)
        h1_t2 = [ABF.alloc(8 * 512) for _ in range(TB // 512)]
        w1s_t = [ABF.alloc(8 * 1024), ABF.alloc(8 * 1024)]
        w2s_t = [ABF.alloc(8 * 1024), ABF.alloc(8 * 1024)]
        oacc_t = AFA.alloc(NT128 * E)
        o1b_t = AFA.alloc(2 * E)

        ev_x2lnT_use = None
        ev_oacc_use = None
        ev_w1s_use = [None, None]
        ev_w2s_use = [None, None]
        ev_o1b_use = [None, None]
        ev_b2t_use = None
        ev_gates_use = None
        ev_h1_use = {t5: None for t5 in range(TB // 512)}
        ev_osl = {}
        ev_out_dma = []

        for b in range(NBLK):
            rows = slice(b * TB, (b + 1) * TB)
            exT = []
            for hc in range(8):
                ev = B.dma("y", out=x2lnT_t[:, hc * TB:(hc + 1) * TB],
                           in_=x2ln_sc.ap()[rows, hc * 128:(hc + 1) * 128],
                           transpose=True, deps=[ev_x2ln_use, ev_x2lnT_use])
                exT.append(ev)
            B.fence("t", exT + [ev_rwT])

            gate_evs = []
            for tt in range(NT128):
                pb = 6 + (tt % 2)
                B.fence("t", [ev_ps[pb]])
                mm = None
                for hc in range(8):
                    mm = B.op("t", "matmul", P[pb][:, 0:8],
                              x2lnT_t[:, hc * TB + tt * 128: hc * TB + (tt + 1) * 128],
                              rwT[:, hc * 8:(hc + 1) * 8],
                              start=(hc == 0), stop=(hc == 7), inc=(hc == 7))
                gsl = gates[:, tt * 8:(tt + 1) * 8]
                g1 = B.op("v", "reduce_max", gcol[:, 0:1], P[pb][:, 0:8], axis=AX.X,
                          deps=[mm, ev_gates_use])
                g2 = B.op("v", "tensor_scalar", gtmp[:, :], P[pb][:, 0:8],
                          gcol[:, 0:1], None, op0=ALU.is_equal, deps=[g1])
                g3 = B.op("v", "scalar_tensor_tensor", gtmp[:, :], gtmp[:, :],
                          -1e30, P[pb][:, 0:8], op0=ALU.mult, op1=ALU.add, deps=[g2])
                g4 = B.op("v", "reduce_max", gcol[:, 1:2], gtmp[:, :], axis=AX.X,
                          deps=[g3])
                g5 = B.op("v", "tensor_scalar", gtmp[:, :], P[pb][:, 0:8],
                          gcol[:, 1:2], None, op0=ALU.is_ge, deps=[g4])
                g6 = B.op("v", "tensor_scalar", gsl, P[pb][:, 0:8],
                          gcol[:, 0:1], None, op0=ALU.subtract, deps=[g5])
                g7 = B.op("s", "activation", gsl, gsl, AF.Exp, deps=[g6])
                g8 = B.op("v", "tensor_tensor", gcol[:, 2:3], gcol[:, 1:2],
                          gcol[:, 0:1], op=ALU.subtract, deps=[g4, g7])
                g9 = B.op("s", "activation", gcol[:, 2:3], gcol[:, 2:3], AF.Exp,
                          deps=[g8])
                g10 = B.op("v", "tensor_scalar", gcol[:, 2:3], gcol[:, 2:3],
                           1.0, None, op0=ALU.add, deps=[g9])
                g11 = B.op("v", "reciprocal", gcol[:, 2:3], gcol[:, 2:3], deps=[g10])
                g12 = B.op("v", "tensor_scalar", gsl, gsl, gcol[:, 2:3], None,
                           op0=ALU.mult, deps=[g11])
                g13 = B.op("v", "tensor_tensor", gsl, gsl, gtmp[:, :],
                           op=ALU.mult, deps=[g12])
                ev_ps[pb] = g13
                gate_evs.append(g13)
            ev_oz = B.op("v", "memset", oacc_t[:, :], 0.0,
                         deps=[ev_oacc_use])
            for _k in list(ev_osl):
                ev_osl[_k] = ev_oz
            ev_gates_use = gate_evs[-1]
            if dbg == "light":
                e_dg = B.dma("y", out=dbg_aps["d_gatesA"][b], in_=gates[:, :],
                             deps=gate_evs)
                ev_gates_use = [gate_evs[-1], e_dg]
            if dbg is True:
                _ed = B.dma("y", out=dbg_aps["d_gates"], in_=gates[:, :], deps=gate_evs)
                B.fence("v", [_ed])
            B.fence("t", gate_evs)

            for e in range(8):
                e_b2l = B.dma("g", out=b2t[:, :], in_=b2_bf.ap()[e][None, :],
                              deps=[e_b2c, ev_b2t_use])
                B.fence("t", [e_b2l, ev_b1T, ev_ones])
                NT5 = TB // 512
                for s in range(4):
                    wb = s % 2
                    e_w1l = B.dma("y", out=w1s_t[wb][:, :],
                                  in_=dram_ap(w1T_sc,
                                              (e * 32 + s) * 128 * 1024,
                                              [(1024, 128), (4 * 128 * 1024, 8),
                                               (1, 1024)]),
                                  deps=[ev_w1T[e], ev_w1s_use[wb]])
                    e_w2l = B.dma("y", out=w2s_t[wb][:, :],
                                  in_=w2T_sc.ap()[e][s * 8:(s + 1) * 8].rearrange(
                                      "fc p h -> p fc h"),
                                  deps=[ev_w2T[e], ev_w2s_use[wb]])
                    gelu_evs = {t5: [] for t5 in range(NT5)}
                    for fcl in range(8):
                        pbs = [2 * (fcl % 2) + t5 for t5 in range(NT5)]
                        fdeps = [e_w1l] + [ev_ps[pb] for pb in pbs]
                        if fcl < 2:
                            fdeps += [ev_h1_use[t5] for t5 in range(NT5)]
                        B.fence("t", fdeps)
                        mmf = {}
                        for hc in range(8):
                            for t5 in range(NT5):
                                mmf[t5] = B.op(
                                    "t", "matmul", P[pbs[t5]][:, 0:512],
                                    w1s_t[wb][:, hc * 1024 + fcl * 128:
                                              hc * 1024 + (fcl + 1) * 128],
                                    x2lnT_t[:, hc * TB + t5 * 512:
                                            hc * TB + t5 * 512 + 512],
                                    start=(hc == 0), stop=(hc == 7),
                                    inc=(hc == 7))
                        fc = s * 8 + fcl
                        for t5 in range(NT5):
                            ge = B.op("s", "activation",
                                      h1_t2[t5][:, fcl * 512:(fcl + 1) * 512],
                                      P[pbs[t5]][:, 0:512], AF.Gelu,
                                      bias=b1T[:, e * 32 + fc: e * 32 + fc + 1],
                                      deps=[mmf[t5]])
                            ev_ps[pbs[t5]] = ge
                            gelu_evs[t5].append(ge)
                    for t5 in range(NT5):
                        B.fence("t", gelu_evs[t5] + [e_w2l])
                        stt_last = None
                        for tt in range(4):
                            pb0 = 4 + 2 * (tt % 2)
                            pb1 = pb0 + 1
                            B.fence("t", [ev_ps[pb0], ev_ps[pb1]])
                            mmh = [None, None]
                            for fcl in range(8):
                                last = (s != 0 and fcl == 7)
                                for half in range(2):
                                    mmh[half] = B.op(
                                        "t", "matmul", P[pb0 + half][:, 0:512],
                                        h1_t2[t5][:, fcl * 512 + tt * 128:
                                                  fcl * 512 + (tt + 1) * 128],
                                        w2s_t[wb][:, fcl * 1024 + half * 512:
                                                  fcl * 1024 + (half + 1) * 512],
                                        start=(fcl == 0), stop=last, inc=last)
                            if s == 0:
                                mmh[0] = B.op("t", "matmul", P[pb0][:, 0:512],
                                              ones1[:, :], b2t[:, 0:512],
                                              start=False, stop=True, inc=True)
                                mmh[1] = B.op("t", "matmul", P[pb1][:, 0:512],
                                              ones1[:, :], b2t[:, 512:1024],
                                              start=False, stop=True, inc=True)
                            t128i = t5 * 4 + tt
                            gc_ap = gates[:, t128i * 8 + e: t128i * 8 + e + 1]
                            for half in range(2):
                                okey = (t128i, half)
                                osl = oacc_t[:, t128i * E + half * 512:
                                             t128i * E + (half + 1) * 512]
                                ev = B.op("v", "scalar_tensor_tensor", osl,
                                          P[pb0 + half][:, 0:512], gc_ap, osl,
                                          op0=ALU.mult, op1=ALU.add,
                                          deps=[mmh[half],
                                                ev_osl.get(okey, ev_oz)])
                                ev_osl[okey] = ev
                                ev_ps[pb0 + half] = ev
                                stt_last = ev
                        ev_h1_use[t5] = stt_last
                    ev_w1s_use[wb] = B.now("t")
                    ev_w2s_use[wb] = B.now("t")
                if dbg == "light" and b == NBLK - 1 and e == 0:
                    e_de0 = B.dma("y", out=dbg_aps["d_e0A"].rearrange(
                        "(a p) h -> p a h", p=128),
                        in_=oacc_t.rearrange("p (a h) -> p a h", a=NT128),
                        deps=[stt_last])
                    B.fence("v", [e_de0])
                ev_b2t_use = B.now("t")
            ev_expert_done = B.now("v")
            if dbg == "light":
                e_doa = B.dma("y", out=dbg_aps["d_oaccA"][rows, :].rearrange(
                    "(a p) h -> p a h", p=128),
                    in_=oacc_t.rearrange("p (a h) -> p a h", a=NT128),
                    deps=[ev_expert_done])
                ev_oacc_use = e_doa

            fin_evs = []
            for tt in range(NT128):
                ob = tt % 2
                r128 = slice(b * TB + tt * 128, b * TB + (tt + 1) * 128)
                e_o1 = B.dma("g", out=o1b_t[:, ob * E:(ob + 1) * E],
                             in_=out1_sc.ap()[r128, :],
                             deps=[ev_out1_use, ev_o1b_use[ob]])
                fa = B.op("v", "tensor_tensor", o1b_t[:, ob * E:(ob + 1) * E],
                          o1b_t[:, ob * E:(ob + 1) * E],
                          oacc_t[:, tt * E:(tt + 1) * E],
                          op=ALU.add, deps=[e_o1, ev_expert_done])
                eo = B.dma("g", out=out_ap[r128, :],
                           in_=o1b_t[:, ob * E:(ob + 1) * E], deps=[fa])
                ev_o1b_use[ob] = eo
                ev_out_dma.append(eo)
                fin_evs.append(fa)
            if dbg == "light":
                ev_oacc_use = [fin_evs[-1], e_doa]
            else:
                ev_oacc_use = fin_evs[-1]
            ev_x2lnT_use = B.now("t")

        if dbg == "light":
            B.dma("g", out=dbg_aps["d_w1sc"], in_=w1T_sc.ap()[0][0],
                  deps=B.everything())
            B.dma("g", out=dbg_aps["d_w1bf"],
                  in_=w1_bf.ap()[0][0:1024, 0:128], deps=B.everything())
        B.fence("y", B.everything())
        B.emit(block)
    for ctx in reversed(ps_ctxs):
        ctx.__exit__(None, None, None)
    for ctx in reversed(sb_ctxs):
        ctx.__exit__(None, None, None)
    B.close()
    return nc



def _consts(L):
    posm1 = np.maximum(np.arange(L, dtype=np.float64) - 1.0, 0.0)
    freqs = 10000.0 ** (-np.arange(0, 64, 2, dtype=np.float64) / 64.0)
    ang = freqs[:, None] * posm1[None, :]
    cos = np.cos(ang)
    sin = np.sin(ang)
    cossin = np.concatenate([cos, sin], axis=0).astype(BF)
    cossin_sw = np.concatenate([sin, cos], axis=0).astype(BF)
    return {
        "id64": np.eye(128, dtype=np.float32),
        "h64c": H64_NP,
        "hpc": HP_NP,
        "cossin": np.ascontiguousarray(cossin),
        "cossin_sw": np.ascontiguousarray(cossin_sw),
    }


def kernel(L0_attn_x, L0_attn_w_qkv, L0_attn_w_out, L0_attn_sorf_signs,
           L0_attn_sorf_scales, L0_ffn_router_w, L0_ffn_w1, L0_ffn_b1,
           L0_ffn_w2, L0_ffn_b2):
    L = L0_attn_x.shape[1]
    nc = build_program(L=L)
    consts = _consts(L)
    shared = {
        "w_qkv": np.ascontiguousarray(L0_attn_w_qkv, dtype=np.float32),
        "w_out": np.ascontiguousarray(L0_attn_w_out, dtype=np.float32),
        "sorf_signs": np.ascontiguousarray(L0_attn_sorf_signs, dtype=np.float32),
        "sorf_scales": np.ascontiguousarray(L0_attn_sorf_scales, dtype=np.float32),
        "router_w": np.ascontiguousarray(L0_ffn_router_w, dtype=np.float32),
        "w1": np.ascontiguousarray(L0_ffn_w1, dtype=np.float32),
        "b1": np.ascontiguousarray(L0_ffn_b1, dtype=np.float32),
        "w2": np.ascontiguousarray(L0_ffn_w2, dtype=np.float32),
        "b2": np.ascontiguousarray(L0_ffn_b2, dtype=np.float32),
        **consts,
    }
    in_maps = []
    for i in range(NCORES):
        m = dict(shared)
        m["x"] = np.ascontiguousarray(L0_attn_x[i], dtype=np.float32)
        in_maps.append(m)
    res = run_bass_kernel_spmd(nc, in_maps, core_ids=list(range(NCORES)))
    return np.stack([r["out"] for r in res.results], axis=0)
